# version 01
# baseline (speedup 1.0000x reference)
import contextlib
import numpy as np
import ml_dtypes
import concourse.bass as bass
import concourse.mybir as mybir
from concourse.bass_utils import run_bass_kernel_spmd

F32 = mybir.dt.float32
BF16 = mybir.dt.bfloat16
AF = mybir.ActivationFunctionType
ALU = mybir.AluOpType
AX = mybir.AxisListType
BFNP = ml_dtypes.bfloat16

ENGINES = ("tensor", "vector", "scalar", "gpsimd", "sync")

B, S, D = 16, 2048, 1024
T = B * S
NCORES = 8
BPC = B // NCORES
TPC = BPC * S
NEG = -1.0e30


class Prog:
    def __init__(self, name="prog"):
        self.nc = bass.Bass("TRN2", target_bir_lowering=False)
        self.stack = contextlib.ExitStack()
        self.sem = {e: self.stack.enter_context(self.nc.semaphore(f"sem_{e}"))
                    for e in ENGINES}
        self.cnt = dict.fromkeys(ENGINES, 0)
        self.waited = {e: dict.fromkeys(ENGINES, 0) for e in ENGINES}
        self.lw = {}
        self.rd = {}
        self.out_sem_target = 0
        self._consts = {}
        self._const_sb = None
        self._const_n = 0

    def sb(self, name, part, free, dt):
        return self.stack.enter_context(self.nc.sbuf_tensor(name, [part, free], dt))

    def ps(self, name, free=512, dt=F32, part=128):
        return self.stack.enter_context(self.nc.psum_tensor(name, [part, free], dt))

    def const(self, value, dt=F32):
        key = (dt, float(value))
        if key in self._consts:
            return value
        if self._const_sb is None:
            self._const_sb = self.stack.enter_context(
                self.nc.sbuf_tensor("const_pool", [128, 64], F32))
        ap = self._const_sb[:, self._const_n:self._const_n + 1]
        self._const_n += 1
        self._emit("vector", self.nc.vector.memset, [ap], [], ap, float(value))
        self.nc.const_aps.register(dt, float(value), ap)
        self._consts[key] = ap
        return value

    @staticmethod
    def _key(ap):
        return ap.tensor.name

    def _wait(self, eng, dep_eng, val):
        if val > self.waited[eng][dep_eng]:
            getattr(self.nc, eng).wait_ge(self.sem[dep_eng], val)
            self.waited[eng][dep_eng] = val

    def _pre(self, eng, outs, ins, skip_waw_tensor=False):
        for ap in ins:
            w = self.lw.get(self._key(ap))
            if w:
                self._wait(eng, w[0], w[1])
        for ap in outs:
            k = self._key(ap)
            w = self.lw.get(k)
            if w and not (skip_waw_tensor and w[0] == "tensor" and eng == "tensor"):
                self._wait(eng, w[0], w[1])
            for e2, v in self.rd.get(k, {}).items():
                self._wait(eng, e2, v)

    def _post(self, eng, outs, ins, step):
        self.cnt[eng] += step
        c = self.cnt[eng]
        for ap in ins:
            self.rd.setdefault(self._key(ap), {})[eng] = c
        for ap in outs:
            k = self._key(ap)
            self.lw[k] = (eng, c)
            self.rd[k] = {}

    def _emit(self, eng, fn, outs, ins, *args, step=1, skip_waw_tensor=False, **kw):
        self._pre(eng, outs, ins, skip_waw_tensor)
        instr = fn(*args, **kw)
        instr.then_inc(self.sem[eng], step)
        self._post(eng, outs, ins, step)
        return instr

    def dma_in(self, dst, src, slow=False):
        kw = {"allow_slow_non_contiguous": True} if slow else {}
        self._emit("sync", self.nc.sync.dma_start, [dst], [src],
                   dst, src, step=16, **kw)

    def dma_out(self, dst, src):
        self._emit("gpsimd", self.nc.gpsimd.dma_start, [dst], [src],
                   dst, src, step=16)
        self.out_sem_target = self.cnt["gpsimd"]

    def mm(self, out, lhsT, rhs, start, stop):
        self._emit("tensor", self.nc.tensor.matmul, [out], [lhsT, rhs],
                   out, lhsT, rhs, start=start, stop=stop,
                   skip_waw_tensor=(not start))

    def trans(self, out, in_, ident):
        self._emit("tensor", self.nc.tensor.transpose, [out], [in_, ident],
                   out, in_, ident)

    def v(self, meth, out, ins, *args, **kw):
        self._emit("vector", getattr(self.nc.vector, meth), [out], list(ins),
                   out, *args, **kw)

    def copy(self, out, in_, eng="vector"):
        if eng == "scalar":
            self._emit("scalar", self.nc.scalar.copy, [out], [in_], out, in_)
        else:
            self.v("tensor_copy", out, [in_], in_)

    def tt(self, out, a, b, op):
        self.v("tensor_tensor", out, [a, b], a, b, op=op)

    def ts(self, out, in_, s1, op0, s2=None, op1=None):
        ins = [in_] + [x for x in (s1, s2) if isinstance(x, bass.AP)]
        if op1 is None:
            self._emit("vector", self.nc.vector.tensor_scalar, [out], ins,
                       out, in_, s1, None, op0)
        else:
            self._emit("vector", self.nc.vector.tensor_scalar, [out], ins,
                       out, in_, s1, s2, op0, op1)

    def act(self, out, in_, func, bias=None, scale=None, accum_out=None):
        kw = {}
        cins = []
        if bias is not None:
            if not isinstance(bias, bass.AP):
                bias = self.const(bias)
                cins.append(self._consts[(F32, float(bias))])
            kw["bias"] = bias
        if scale is not None:
            if not isinstance(scale, bass.AP):
                scale = self.const(scale)
                cins.append(self._consts[(F32, float(scale))])
            kw["scale"] = scale
        if accum_out is not None:
            kw["accum_out"] = accum_out
        ins = [in_] + ([bias] if isinstance(bias, bass.AP) else []) + cins
        outs = [out] + ([accum_out] if accum_out is not None else [])
        self._pre("scalar", outs, ins)
        instr = self.nc.scalar.activation(out, in_, func, **kw)
        instr.then_inc(self.sem["scalar"], 1)
        self._post("scalar", outs, ins, 1)

    def smul(self, out, in_, c):
        self._emit("scalar", self.nc.scalar.mul, [out], [in_], out, in_, c)

    def red(self, out, in_, op, negate=False):
        self._emit("vector", self.nc.vector.tensor_reduce, [out], [in_],
                   out, in_, axis=AX.X, op=op, negate=negate)

    def scan(self, out, a, b, initial=0.0):
        ins = [a, b] + ([initial] if isinstance(initial, bass.AP) else [])
        self._emit("vector", self.nc.vector.tensor_tensor_scan, [out], ins,
                   out, a, b, initial, ALU.mult, ALU.add)

    def memset(self, ap, val):
        self._emit("vector", self.nc.vector.memset, [ap], [], ap, float(val))

    def recip(self, out, in_):
        self.v("reciprocal", out, [in_], in_)

    def finalize(self):
        if self.out_sem_target:
            self.nc.vector.wait_ge(self.sem["gpsimd"], self.out_sem_target)
        if self.cnt["sync"]:
            self.nc.vector.wait_ge(self.sem["sync"], self.cnt["sync"])


def ln_into(p, xstage, xc, sq, st, xn_out):
    p.red(st[:, 0:1], xstage, ALU.add)
    p.smul(st[:, 1:2], st[:, 0:1], 1.0 / 1024)
    p.ts(xc, xstage, st[:, 1:2], ALU.subtract)
    p.act(sq, xc, AF.Square, accum_out=st[:, 2:3])
    p.act(st[:, 3:4], st[:, 2:3], AF.Sqrt, bias=1e-5, scale=1.0 / 1024)
    p.recip(st[:, 4:5], st[:, 3:4])
    p.ts(xn_out, xc, st[:, 4:5], ALU.mult)


_PROGS = {}


def get_prog(name, builder):
    if name not in _PROGS:
        _PROGS[name] = builder()
    return _PROGS[name]


IDENTF = np.eye(128, dtype=np.float32)
IDENTB = IDENTF.astype(BFNP)



def build_mingru():
    p = Prog("mingru")
    nc = p.nc
    xs = nc.dram_tensor("xs", [TPC, D], F32, kind="ExternalInput").ap()
    whgT = nc.dram_tensor("whgT", [1024, 3072], F32, kind="ExternalInput").ap()
    woutT = nc.dram_tensor("woutT", [1536, 1024], F32, kind="ExternalInput").ap()
    identbd = nc.dram_tensor("identb", [128, 128], BF16, kind="ExternalInput").ap()
    ys = nc.dram_tensor("ys", [TPC, D], F32, kind="ExternalOutput").ap()

    xstage = p.sb("xstage", 128, 2048, F32)
    xc = p.sb("xc", 128, 1024, F32)
    sq = p.sb("sq", 128, 1024, F32)
    st = p.sb("st", 128, 8, F32)
    xn = p.sb("xn", 128, 1024, BF16)
    xnT = p.sb("xnT", 128, 8 * 2048, BF16)
    identb = p.sb("identb_s", 128, 128, BF16)
    wstage = p.sb("wstage", 128, 2048, F32)
    whg_h = p.sb("whg_h", 128, 8 * 128, BF16)
    whg_g = p.sb("whg_g", 128, 8 * 128, BF16)
    wout_sb = p.sb("wout_sb", 128, 12 * 1024, BF16)
    a_sb = p.sb("a_sb", 128, 2048, BF16)
    b_sb = p.sb("b_sb", 128, 2048, BF16)
    hf = p.sb("hf", 128, 2048, BF16)
    hb = p.sb("hb", 128, 2048, BF16)
    hsum = p.sb("hsum", 128, 12 * 2048, BF16)
    t0f = p.sb("t0f", 128, 512, F32)
    t1f = p.sb("t1f", 128, 512, F32)
    t2f = p.sb("t2f", 128, 512, F32)
    t3f = p.sb("t3f", 128, 512, F32)
    oevac = p.sb("oevac", 128, 2 * 512, BF16)
    otile = p.sb("otile", 128, 4 * 1024, F32)

    ps_h = p.ps("ps_h")
    ps_g = p.ps("ps_g")
    psT = [p.ps("psT0", 1024, BF16), p.ps("psT1", 1024, BF16)]
    ps_o = [p.ps("ps_o0"), p.ps("ps_o1")]

    p.dma_in(identb[:], identbd)
    for kt in range(12):
        p.dma_in(wstage[:, 0:1024], woutT[kt * 128:(kt + 1) * 128, :])
        p.copy(wout_sb[:, kt * 1024:(kt + 1) * 1024], wstage[:, 0:1024])

    for b in range(BPC):
        base = b * S
        for tile in range(16):
            t0 = base + tile * 128
            xsl = xstage[:, (tile % 2) * 1024:(tile % 2) * 1024 + 1024]
            p.dma_in(xsl, xs[t0:t0 + 128, :])
            ln_into(p, xsl, xc[:], sq[:], st[:], xn[:, :])
            for kt in range(8):
                pt = psT[kt % 2][:, (kt % 2) * 0:128]
                p.trans(pt, xn[:, kt * 128:(kt + 1) * 128], identb[:])
                p.copy(xnT[:, kt * 2048 + tile * 128: kt * 2048 + tile * 128 + 128], pt)
        for ft in range(12):
            for kt in range(8):
                p.dma_in(wstage[:, 0:128], whgT[kt * 128:(kt + 1) * 128,
                                                ft * 128:(ft + 1) * 128])
                p.copy(whg_h[:, kt * 128:(kt + 1) * 128], wstage[:, 0:128])
                p.dma_in(wstage[:, 128:256], whgT[kt * 128:(kt + 1) * 128,
                                                  1536 + ft * 128:1536 + ft * 128 + 128])
                p.copy(whg_g[:, kt * 128:(kt + 1) * 128], wstage[:, 128:256])
            for tn in range(4):
                for kt in range(8):
                    p.mm(ps_h[:, :], whg_h[:, kt * 128:(kt + 1) * 128],
                         xnT[:, kt * 2048 + tn * 512: kt * 2048 + tn * 512 + 512],
                         start=(kt == 0), stop=(kt == 7))
                for kt in range(8):
                    p.mm(ps_g[:, :], whg_g[:, kt * 128:(kt + 1) * 128],
                         xnT[:, kt * 2048 + tn * 512: kt * 2048 + tn * 512 + 512],
                         start=(kt == 0), stop=(kt == 7))
                sl = slice(tn * 512, tn * 512 + 512)
                p.act(a_sb[:, sl], ps_g[:, :], AF.Sigmoid, scale=-1.0)
                p.act(t0f[:, :], ps_g[:, :], AF.Sigmoid)
                p.ts(t1f[:, :], ps_h[:, :], 0.0, ALU.min)
                p.act(t2f[:, :], t1f[:, :], AF.Sigmoid)
                p.ts(t3f[:, :], ps_h[:, :], 0.0, ALU.max)
                p.tt(t2f[:, :], t2f[:, :], t3f[:, :], ALU.add)
                p.tt(b_sb[:, sl], t0f[:, :], t2f[:, :], ALU.mult)
            p.scan(hf[:, :], a_sb[:, :], b_sb[:, :], 0.0)
            p.scan(hb[:, :], a_sb[:, ::-1], b_sb[:, ::-1], 0.0)
            p.tt(hsum[:, ft * 2048:(ft + 1) * 2048], hf[:, :], hb[:, :], ALU.add)
        for tn in range(4):
            for dt in range(8):
                po = ps_o[dt % 2]
                for kt in range(12):
                    p.mm(po[:, :], wout_sb[:, kt * 1024 + dt * 128: kt * 1024 + dt * 128 + 128],
                         hsum[:, kt * 2048 + tn * 512: kt * 2048 + tn * 512 + 512],
                         start=(kt == 0), stop=(kt == 11))
                ev = oevac[:, (dt % 2) * 512:(dt % 2) * 512 + 512]
                p.copy(ev, po[:, :])
                for sub in range(4):
                    pt = psT[sub % 2][:, 128:256]
                    p.trans(pt, ev[:, sub * 128:sub * 128 + 128], identb[:])
                    p.copy(otile[:, sub * 1024 + dt * 128: sub * 1024 + dt * 128 + 128], pt)
            for sub in range(4):
                t0 = base + tn * 512 + sub * 128
                xsl = xstage[:, (sub % 2) * 1024:(sub % 2) * 1024 + 1024]
                p.dma_in(xsl, xs[t0:t0 + 128, :])
                osl = otile[:, sub * 1024:(sub + 1) * 1024]
                p.tt(osl, osl, xsl, ALU.add)
                p.dma_out(ys[t0:t0 + 128, :], osl)
    p.finalize()

    def run(x, inp):
        whgT_h = np.ascontiguousarray(inp["L0_attn_a_W_hg"].T)
        woutT_h = np.ascontiguousarray(inp["L0_attn_a_W_out"].T)
        in_maps = [{"xs": x[c * TPC:(c + 1) * TPC], "whgT": whgT_h,
                    "woutT": woutT_h, "identb": IDENTB} for c in range(NCORES)]
        res = run_bass_kernel_spmd(nc, in_maps, core_ids=list(range(NCORES)))
        return np.concatenate([np.asarray(r["ys"]) for r in res.results], axis=0)

    return run



def build_moe(layer):
    NF = 2048 if layer == 0 else 4096
    NFT = NF // 128
    CAP = float(int(T * 1.25 / 8))
    p = Prog(f"moe{layer}")
    nc = p.nc
    xall = nc.dram_tensor("xall", [T, D], F32, kind="ExternalInput").ap()
    wrd = nc.dram_tensor("wr", [1024, 8], F32, kind="ExternalInput").ap()
    identfd = nc.dram_tensor("identf", [128, 128], F32, kind="ExternalInput").ap()
    identbd = nc.dram_tensor("identb", [128, 128], BF16, kind="ExternalInput").ap()
    eohd = nc.dram_tensor("eoh", [128, 8], F32, kind="ExternalInput").ap()
    wgd = nc.dram_tensor("wg", [1024, NF], F32, kind="ExternalInput").ap()
    wdd = nc.dram_tensor("wd", [NF, 1024], F32, kind="ExternalInput").ap()
    if layer == 0:
        wud = nc.dram_tensor("wu", [1024, NF], F32, kind="ExternalInput").ap()
    else:
        b1d = nc.dram_tensor("b1", [128, NFT], F32, kind="ExternalInput").ap()
        b2d = nc.dram_tensor("b2", [128, 8], F32, kind="ExternalInput").ap()
        ltrid = nc.dram_tensor("ltri", [128, 128], F32, kind="ExternalInput").ap()
    part = nc.dram_tensor("part", [T, D], BF16, kind="ExternalOutput").ap()

    xstage = p.sb("xstage", 128, 1024, F32)
    xc = p.sb("xc", 128, 1024, F32)
    sq = p.sb("sq", 128, 1024, F32)
    st = p.sb("st", 128, 8, F32)
    xn32 = p.sb("xn32", 128, 1024, F32)
    xnT32t = p.sb("xnT32t", 128, 8 * 128, F32)
    xnTb = p.sb("xnTb", 128, 8 * 512, BF16)
    identf = p.sb("identf_s", 128, 128, F32)
    identb = p.sb("identb_s", 128, 128, BF16)
    eoh = p.sb("eoh_s", 128, 8, F32)
    wr_sb = p.sb("wr_sb", 128, 64, F32)
    wstage = p.sb("wstage", 128, 2048, F32)
    wg_sb = p.sb("wg_sb", 128, 8 * NF, BF16)
    wd_sb = p.sb("wd_sb", 128, NFT * 1024, BF16)
    if layer == 0:
        wu_sb = p.sb("wu_sb", 128, 8 * NF, BF16)
        tsl = p.sb("tsl", 128, 512, F32)
    else:
        b1_sb = p.sb("b1_sb", 128, NFT, F32)
        b2_sb = p.sb("b2_sb", 128, 8, F32)
        ltri = p.sb("ltri_s", 128, 128, F32)
        onescol = p.sb("onescol", 128, 1, F32)
        onesrow = p.sb("onesrow", 1, 128, F32)
        carry = p.sb("carry", 1, 2, F32)
        ind = p.sb("ind", 128, 2, F32)
        keep = p.sb("keep", 128, 2, F32)
    f_sb = p.sb("f_sb", 128, NFT * 512, BF16)
    ex = p.sb("ex", 128, 8, F32)
    ex2 = p.sb("ex2", 128, 8, F32)
    mask1 = p.sb("mask1", 128, 8, F32)
    mask2 = p.sb("mask2", 128, 8, F32)
    rtmp = p.sb("rtmp", 128, 8, F32)
    nw1 = p.sb("nw1", 128, 1, F32)
    nw2 = p.sb("nw2", 128, 1, F32)
    wa = p.sb("wa", 128, 1, F32)
    wb = p.sb("wb", 128, 1, F32)
    wcol = p.sb("wcol", 128, 4, F32)
    dstage = p.sb("dstage", 128, 2 * 512, BF16)
    dtmp = p.sb("dtmp", 128, 2 * 128, BF16)

    ps_g = p.ps("ps_g")
    ps_d = [p.ps("ps_d0"), p.ps("ps_d1")]
    ps_r = p.ps("ps_r", 512, F32)
    psT = [p.ps("psT0", 1024, BF16), p.ps("psT1", 1024, BF16)]
    psT32 = p.ps("psT32", 512, F32)
    if layer == 0:
        ps_u = p.ps("ps_u")
    else:
        ps_c = p.ps("ps_c", 512, F32)

    p.dma_in(identf[:], identfd)
    p.dma_in(identb[:], identbd)
    p.dma_in(eoh[:], eohd)
    for kt in range(8):
        p.dma_in(wr_sb[:, kt * 8:(kt + 1) * 8], wrd[kt * 128:(kt + 1) * 128, :])
    for kt in range(8):
        for pc in range(NF // 2048):
            p.dma_in(wstage[:, 0:2048], wgd[kt * 128:(kt + 1) * 128,
                                            pc * 2048:(pc + 1) * 2048])
            p.copy(wg_sb[:, kt * NF + pc * 2048:kt * NF + (pc + 1) * 2048],
                   wstage[:, 0:2048])
    if layer == 0:
        for kt in range(8):
            p.dma_in(wstage[:, 0:NF], wud[kt * 128:(kt + 1) * 128, :])
            p.copy(wu_sb[:, kt * NF:(kt + 1) * NF], wstage[:, 0:NF])
    for kt in range(NFT):
        p.dma_in(wstage[:, 0:1024], wdd[kt * 128:(kt + 1) * 128, :])
        p.copy(wd_sb[:, kt * 1024:(kt + 1) * 1024], wstage[:, 0:1024])
    if layer == 1:
        p.dma_in(b1_sb[:], b1d)
        p.dma_in(b2_sb[:], b2d)
        p.dma_in(ltri[:], ltrid)
        p.memset(onescol[:], 1.0)
        p.memset(onesrow[:], 1.0)
        p.memset(carry[:], 0.0)

    for ch in range(64):
        c0 = ch * 512
        for sub in range(4):
            t0 = c0 + sub * 128
            p.dma_in(xstage[:], xall[t0:t0 + 128, :])
            ln_into(p, xstage[:], xc[:], sq[:], st[:], xn32[:])
            for kt in range(8):
                p.trans(psT32[:, 0:128], xn32[:, kt * 128:(kt + 1) * 128], identf[:])
                p.copy(xnT32t[:, kt * 128:(kt + 1) * 128], psT32[:, 0:128])
                p.copy(xnTb[:, kt * 512 + sub * 128: kt * 512 + sub * 128 + 128],
                       xnT32t[:, kt * 128:(kt + 1) * 128])
            for kt in range(8):
                p.mm(ps_r[:, 0:8], xnT32t[:, kt * 128:(kt + 1) * 128],
                     wr_sb[:, kt * 8:(kt + 1) * 8], start=(kt == 0), stop=(kt == 7))
            p.red(st[:, 5:6], ps_r[:, 0:8], ALU.max, negate=True)
            p.act(ex[:], ps_r[:, 0:8], AF.Exp, bias=st[:, 5:6])
            p.red(st[:, 6:7], ex[:], ALU.add)
            p.red(st[:, 7:8], ex[:], ALU.max)
            p.ts(mask1[:], ex[:], st[:, 7:8], ALU.is_ge)
            p.ts(rtmp[:], mask1[:], 1e30, ALU.mult)
            p.tt(ex2[:], ex[:], rtmp[:], ALU.subtract)
            p.red(st[:, 0:1], ex2[:], ALU.max)
            p.ts(mask2[:], ex2[:], st[:, 0:1], ALU.is_ge)
            if layer == 0:
                p.tt(nw1[:], st[:, 7:8], st[:, 0:1], ALU.subtract)
                p.recip(nw2[:], st[:, 6:7])
                p.tt(nw1[:], nw1[:], nw2[:], ALU.mult)
                p.act(nw1[:], nw1[:], AF.Sigmoid)
                p.ts(nw2[:], nw1[:], -1.0, ALU.mult, 1.0, ALU.add)
            else:
                p.tt(nw1[:], st[:, 7:8], st[:, 0:1], ALU.add)
                p.recip(nw1[:], nw1[:])
                p.tt(nw2[:], st[:, 0:1], nw1[:], ALU.mult)
                p.tt(nw1[:], st[:, 7:8], nw1[:], ALU.mult)
            p.tt(rtmp[:], mask1[:], eoh[:], ALU.mult)
            p.red(wa[:], rtmp[:], ALU.add)
            p.tt(rtmp[:], mask2[:], eoh[:], ALU.mult)
            p.red(wb[:], rtmp[:], ALU.add)
            if layer == 0:
                p.tt(wa[:], wa[:], nw1[:], ALU.mult)
                p.tt(wb[:], wb[:], nw2[:], ALU.mult)
                p.tt(wcol[:, sub:sub + 1], wa[:], wb[:], ALU.add)
            else:
                p.copy(ind[:, 0:1], wa[:])
                p.copy(ind[:, 1:2], wb[:])
                p.mm(ps_c[:, 0:2], ltri[:], ind[:], start=True, stop=False)
                p.mm(ps_c[:, 0:2], onesrow[:], carry[:], start=False, stop=True)
                p.ts(keep[:], ps_c[:, 0:2], CAP, ALU.is_le)
                p.tt(wa[:], wa[:], nw1[:], ALU.mult)
                p.tt(wa[:], wa[:], keep[:, 0:1], ALU.mult)
                p.tt(wb[:], wb[:], nw2[:], ALU.mult)
                p.tt(wb[:], wb[:], keep[:, 1:2], ALU.mult)
                p.tt(wcol[:, sub:sub + 1], wa[:], wb[:], ALU.add)
                p.mm(ps_c[0:1, 4:6], onescol[:], ind[:], start=True, stop=True)
                p.tt(carry[:], carry[:], ps_c[0:1, 4:6], ALU.add)
        for ft in range(NFT):
            for kt in range(8):
                p.mm(ps_g[:, :], wg_sb[:, kt * NF + ft * 128: kt * NF + ft * 128 + 128],
                     xnTb[:, kt * 512:(kt + 1) * 512], start=(kt == 0), stop=(kt == 7))
            if layer == 0:
                for kt in range(8):
                    p.mm(ps_u[:, :], wu_sb[:, kt * NF + ft * 128: kt * NF + ft * 128 + 128],
                         xnTb[:, kt * 512:(kt + 1) * 512], start=(kt == 0), stop=(kt == 7))
                p.act(tsl[:], ps_g[:, :], AF.Silu)
                p.tt(f_sb[:, ft * 512:(ft + 1) * 512], tsl[:], ps_u[:, :], ALU.mult)
            else:
                p.act(f_sb[:, ft * 512:(ft + 1) * 512], ps_g[:, :], AF.Gelu,
                      bias=b1_sb[:, ft:ft + 1])
        for dt in range(8):
            po = ps_d[dt % 2]
            for kt in range(NFT):
                p.mm(po[:, :], wd_sb[:, kt * 1024 + dt * 128: kt * 1024 + dt * 128 + 128],
                     f_sb[:, kt * 512:(kt + 1) * 512], start=(kt == 0), stop=(kt == NFT - 1))
            dsl = dstage[:, (dt % 2) * 512:(dt % 2) * 512 + 512]
            if layer == 1:
                p.ts(dsl, po[:, :], b2_sb[:, dt:dt + 1], ALU.add)
            else:
                p.copy(dsl, po[:, :])
            for sub in range(4):
                pt = psT[sub % 2][:, 0:128]
                p.trans(pt, dsl[:, sub * 128:sub * 128 + 128], identb[:])
                dm = dtmp[:, (sub % 2) * 128:(sub % 2) * 128 + 128]
                p.copy(dm, pt)
                p.ts(dm, dm, wcol[:, sub:sub + 1], ALU.mult)
                p.dma_out(part[c0 + sub * 128:c0 + sub * 128 + 128,
                               dt * 128:(dt + 1) * 128], dm)
    p.finalize()

    def run(x, inp, pfx):
        wr_h = inp[f"{pfx}_router_kernel_DE" if layer == 0 else f"{pfx}_Wr"]
        in_maps = []
        for e in range(NCORES):
            eoh_h = np.zeros((128, 8), np.float32)
            eoh_h[:, e] = 1.0
            m = {"xall": x, "wr": np.ascontiguousarray(wr_h),
                 "identf": IDENTF, "identb": IDENTB, "eoh": eoh_h}
            if layer == 0:
                m["wg"] = np.ascontiguousarray(inp[f"{pfx}_kernel_gating_EDF"][e])
                m["wu"] = np.ascontiguousarray(inp[f"{pfx}_kernel_up_EDF"][e])
                m["wd"] = np.ascontiguousarray(inp[f"{pfx}_kernel_down_EFD"][e])
            else:
                m["wg"] = np.ascontiguousarray(inp[f"{pfx}_W1"][e])
                m["wd"] = np.ascontiguousarray(inp[f"{pfx}_W2"][e])
                m["b1"] = np.ascontiguousarray(
                    np.asarray(inp[f"{pfx}_b1"][e]).reshape(NFT, 128).T)
                m["b2"] = np.ascontiguousarray(
                    np.asarray(inp[f"{pfx}_b2"][e]).reshape(8, 128).T)
                m["ltri"] = np.ascontiguousarray(
                    np.tril(np.ones((128, 128), np.float32)).T)
            in_maps.append(m)
        res = run_bass_kernel_spmd(nc, in_maps, core_ids=list(range(NCORES)))
        return [np.asarray(r["part"]) for r in res.results]

    return run



def build_moesum():
    p = Prog("moesum")
    nc = p.nc
    xs = nc.dram_tensor("xs", [TPC, D], F32, kind="ExternalInput").ap()
    pd = [nc.dram_tensor(f"p{e}", [TPC, D], BF16, kind="ExternalInput").ap()
          for e in range(8)]
    ys = nc.dram_tensor("ys", [TPC, D], F32, kind="ExternalOutput").ap()
    xstage = p.sb("xstage", 128, 2 * 1024, F32)
    pstage = p.sb("pstage", 128, 8 * 1024, BF16)
    acc = p.sb("acc", 128, 2 * 1024, F32)
    for tile in range(TPC // 128):
        t0 = tile * 128
        i = tile % 2
        xsl = xstage[:, i * 1024:(i + 1) * 1024]
        asl = acc[:, i * 1024:(i + 1) * 1024]
        p.dma_in(xsl, xs[t0:t0 + 128, :])
        for e in range(8):
            p.dma_in(pstage[:, e * 1024:(e + 1) * 1024], pd[e][t0:t0 + 128, :])
        p.tt(asl, xsl, pstage[:, 0:1024], ALU.add)
        for e in range(1, 8):
            p.tt(asl, asl, pstage[:, e * 1024:(e + 1) * 1024], ALU.add)
        p.dma_out(ys[t0:t0 + 128, :], asl)
    p.finalize()

    def run(x, parts):
        in_maps = []
        for c in range(NCORES):
            m = {"xs": x[c * TPC:(c + 1) * TPC]}
            for e in range(8):
                m[f"p{e}"] = parts[e][c * TPC:(c + 1) * TPC]
            in_maps.append(m)
        res = run_bass_kernel_spmd(nc, in_maps, core_ids=list(range(NCORES)))
        return np.concatenate([np.asarray(r["ys"]) for r in res.results], axis=0)

    return run


def build_attn0():
    p = Prog("attn0")
    nc = p.nc
    xs = nc.dram_tensor("xs", [TPC, D], F32, kind="ExternalInput").ap()
    wqkvT = nc.dram_tensor("wqkvT", [1024, 3088], F32, kind="ExternalInput").ap()
    qkbd = nc.dram_tensor("qkb", [128, 2048], BF16, kind="ExternalInput").ap()
    vbd = nc.dram_tensor("vb", [128, 1040], BF16, kind="ExternalInput").ap()
    cosd = nc.dram_tensor("cosE", [2048, 256], F32, kind="ExternalInput").ap()
    sind = nc.dram_tensor("sinE", [2048, 256], F32, kind="ExternalInput").ap()
    leped = nc.dram_tensor("lepe", [128, 32], F32, kind="ExternalInput").ap()
    identbd = nc.dram_tensor("identb", [128, 128], BF16, kind="ExternalInput").ap()
    ys = nc.dram_tensor("ys", [TPC, D], F32, kind="ExternalOutput").ap()

    xstage = p.sb("xstage", 128, 1024, F32)
    xc = p.sb("xc", 128, 1024, F32)
    sq = p.sb("sq", 128, 1024, F32)
    st = p.sb("st", 128, 8, F32)
    xn = p.sb("xn", 128, 1024, BF16)
    xnTt = p.sb("xnTt", 128, 8 * 128, BF16)
    q_sb = p.sb("q_sb", 128, 16 * 512, BF16)
    k_sb = p.sb("k_sb", 128, 16 * 512, BF16)
    v_sb = p.sb("v_sb", 128, 16 * 520, BF16)
    vT = p.sb("vT", 128, 8 * 2048, BF16)
    outT = p.sb("outT", 128, 8 * 2048, BF16)
    pT = [p.sb("pT0", 128, 512, BF16), p.sb("pT1", 128, 512, BF16)]
    qTh = p.sb("qTh", 64, 2048, BF16)
    kTh = p.sb("kTh", 64, 2048, BF16)
    ct_sb = p.sb("ct_sb", 128, 256, F32)
    st_sb = p.sb("st_sb", 128, 256, F32)
    qkb = p.sb("qkb_s", 128, 2048, BF16)
    vb = p.sb("vb_s", 128, 1040, BF16)
    lepe = p.sb("lepe_s", 128, 32, F32)
    wq_sb = p.sb("wq_sb", 128, 8 * 512, BF16)
    wk_sb = p.sb("wk_sb", 128, 8 * 512, BF16)
    wv_sb = p.sb("wv_sb", 128, 8 * 520, BF16)
    wstage = p.sb("wstage", 128, 1024, F32)
    stage = p.sb("stage", 128, 512, F32)
    tmpE = p.sb("tmpE", 128, 512, F32)
    tmpR = p.sb("tmpR", 128, 512, F32)
    tA = p.sb("tA", 128, 256, F32)
    tB = p.sb("tB", 128, 256, F32)
    lstage = p.sb("lstage", 128, 512, F32)
    ltmp = p.sb("ltmp", 128, 512, F32)
    rd_sb = p.sb("rd_sb", 1, 512, F32)
    rdb = p.sb("rdb", 1, 512, BF16)
    onesb = p.sb("onesb", 1, 64, BF16)
    bc_sb = p.sb("bc_sb", 64, 512, BF16)
    otile = p.sb("otile", 128, 1024, F32)
    identb = p.sb("identb_s", 128, 128, BF16)

    ps_a = p.ps("ps_a")
    ps_b = p.ps("ps_b")
    ps_s = [p.ps("ps_s0"), p.ps("ps_s1")]
    ps_o = p.ps("ps_o")
    ps_bc = p.ps("ps_bc")
    psT = [p.ps("psT0", 1024, BF16), p.ps("psT1", 1024, BF16)]

    p.dma_in(identb[:], identbd)
    p.dma_in(qkb[:], qkbd)
    p.dma_in(vb[:], vbd)
    p.dma_in(lepe[:], leped)
    p.memset(onesb[:], 1.0)

    def rope_block(dst_base, src):
        p.tt(tA[:], src[:, 0::2], ct_sb[:], ALU.mult)
        p.tt(tB[:], src[:, 1::2], st_sb[:], ALU.mult)
        p.tt(dst_base[:, 0::2], tA[:], tB[:], ALU.subtract)
        p.tt(tA[:], src[:, 0::2], st_sb[:], ALU.mult)
        p.tt(tB[:], src[:, 1::2], ct_sb[:], ALU.mult)
        p.tt(dst_base[:, 1::2], tA[:], tB[:], ALU.add)

    for b in range(BPC):
        base = b * S
        for half in range(2):
            hoff = half * 512
            for kt in range(8):
                r = slice(kt * 128, (kt + 1) * 128)
                p.dma_in(wstage[:, 0:512], wqkvT[r, hoff:hoff + 512])
                p.copy(wq_sb[:, kt * 512:(kt + 1) * 512], wstage[:, 0:512])
                p.dma_in(wstage[:, 0:512], wqkvT[r, 1024 + hoff:1024 + hoff + 512])
                p.copy(wk_sb[:, kt * 512:(kt + 1) * 512], wstage[:, 0:512])
                p.dma_in(wstage[:, 0:520], wqkvT[r, 2048 + 520 * half:2048 + 520 * half + 520])
                p.copy(wv_sb[:, kt * 520:(kt + 1) * 520], wstage[:, 0:520])
            for tile in range(16):
                t0 = base + tile * 128
                p.dma_in(xstage[:], xs[t0:t0 + 128, :])
                ln_into(p, xstage[:], xc[:], sq[:], st[:], xn[:])
                for kt in range(8):
                    pt = psT[kt % 2][:, 0:128]
                    p.trans(pt, xn[:, kt * 128:(kt + 1) * 128], identb[:])
                    p.copy(xnTt[:, kt * 128:(kt + 1) * 128], pt)
                p.dma_in(ct_sb[:], cosd[tile * 128:(tile + 1) * 128, :])
                p.dma_in(st_sb[:], sind[tile * 128:(tile + 1) * 128, :])
                for kt in range(8):
                    p.mm(ps_a[:, :], xnTt[:, kt * 128:(kt + 1) * 128],
                         wq_sb[:, kt * 512:(kt + 1) * 512], start=(kt == 0), stop=(kt == 7))
                p.tt(stage[:], ps_a[:, :], qkb[:, hoff:hoff + 512], ALU.add)
                p.ts(tmpE[:], stage[:], 0.0, ALU.min)
                p.act(tmpE[:], tmpE[:], AF.Exp)
                p.ts(tmpR[:], stage[:], 0.0, ALU.max)
                p.tt(stage[:], tmpE[:], tmpR[:], ALU.add)
                rope_block(q_sb[:, tile * 512:(tile + 1) * 512], stage)
                for kt in range(8):
                    p.mm(ps_b[:, :], xnTt[:, kt * 128:(kt + 1) * 128],
                         wk_sb[:, kt * 512:(kt + 1) * 512], start=(kt == 0), stop=(kt == 7))
                p.tt(stage[:], ps_b[:, :], qkb[:, 1024 + hoff:1024 + hoff + 512], ALU.add)
                p.ts(tmpE[:], stage[:], 0.0, ALU.min)
                p.act(tmpE[:], tmpE[:], AF.Exp)
                p.ts(tmpR[:], stage[:], 0.0, ALU.max)
                p.tt(stage[:], tmpE[:], tmpR[:], ALU.add)
                rope_block(k_sb[:, tile * 512:(tile + 1) * 512], stage)
                for kt in range(8):
                    p.mm(ps_a[:, 0:512], xnTt[:, kt * 128:(kt + 1) * 128],
                         wv_sb[:, kt * 520:kt * 520 + 512], start=(kt == 0), stop=(kt == 7))
                for kt in range(8):
                    p.mm(ps_b[:, 0:8], xnTt[:, kt * 128:(kt + 1) * 128],
                         wv_sb[:, kt * 520 + 512:kt * 520 + 520], start=(kt == 0), stop=(kt == 7))
                p.tt(v_sb[:, tile * 520:tile * 520 + 512], ps_a[:, 0:512],
                     vb[:, 520 * half:520 * half + 512], ALU.add)
                p.tt(v_sb[:, tile * 520 + 512:tile * 520 + 520], ps_b[:, 0:8],
                     vb[:, 520 * half + 512:520 * half + 520], ALU.add)
                for h in range(8):
                    hh = half * 8 + h
                    pt = psT[h % 2][:, 128:256]
                    p.trans(pt[0:64, :], v_sb[:, tile * 520 + h * 65:tile * 520 + h * 65 + 64],
                            identb[:])
                    p.copy(vT[64 * (hh % 2):64 * (hh % 2) + 64,
                             (hh // 2) * 2048 + tile * 128:(hh // 2) * 2048 + tile * 128 + 128],
                           pt[0:64, :])
            for h in range(8):
                hh = half * 8 + h
                for tile in range(16):
                    pt = psT[tile % 2][:, 256:384]
                    p.trans(pt[0:64, :], q_sb[:, tile * 512 + h * 64:tile * 512 + h * 64 + 64],
                            identb[:])
                    p.copy(qTh[:, tile * 128:(tile + 1) * 128], pt[0:64, :])
                    pt2 = psT[tile % 2][:, 384:512]
                    p.trans(pt2[0:64, :], k_sb[:, tile * 512 + h * 64:tile * 512 + h * 64 + 64],
                            identb[:])
                    p.copy(kTh[:, tile * 128:(tile + 1) * 128], pt2[0:64, :])
                for qc in range(4):
                    for kt in range(16):
                        pss = ps_s[kt % 2]
                        p.mm(pss[:, :], kTh[:, kt * 128:(kt + 1) * 128],
                             qTh[:, qc * 512:(qc + 1) * 512], start=True, stop=True)
                        p.act(pT[kt % 2][:, :], pss[:, :], AF.Exp, bias=-20.0, scale=0.125)
                        p.mm(ps_o[0:65, :], v_sb[:, kt * 520 + h * 65:kt * 520 + h * 65 + 65],
                             pT[kt % 2][:, :], start=(kt == 0), stop=(kt == 15))
                    p.recip(rd_sb[0:1, :], ps_o[64:65, :])
                    p.copy(rdb[0:1, :], rd_sb[0:1, :])
                    p.mm(ps_bc[0:64, :], onesb[:, :], rdb[0:1, :], start=True, stop=True)
                    p.copy(bc_sb[:, :], ps_bc[0:64, :])
                    p.tt(outT[64 * (hh % 2):64 * (hh % 2) + 64,
                              (hh // 2) * 2048 + qc * 512:(hh // 2) * 2048 + qc * 512 + 512],
                         ps_o[0:64, :], bc_sb[:, :], ALU.mult)
        for qc in range(4):
            for ct in range(8):
                vsl = vT[:, ct * 2048 + qc * 512: ct * 2048 + qc * 512 + 512]
                osl = outT[:, ct * 2048 + qc * 512: ct * 2048 + qc * 512 + 512]
                p.ts(lstage[:], vsl, lepe[:, ct * 4 + 1:ct * 4 + 2], ALU.mult,
                     lepe[:, ct * 4 + 3:ct * 4 + 4], ALU.add)
                if qc == 0:
                    p.ts(ltmp[:, 0:511], vT[:, ct * 2048:ct * 2048 + 511],
                         lepe[:, ct * 4 + 0:ct * 4 + 1], ALU.mult)
                    p.tt(lstage[:, 1:512], lstage[:, 1:512], ltmp[:, 0:511], ALU.add)
                else:
                    p.ts(ltmp[:], vT[:, ct * 2048 + qc * 512 - 1: ct * 2048 + qc * 512 + 511],
                         lepe[:, ct * 4 + 0:ct * 4 + 1], ALU.mult)
                    p.tt(lstage[:], lstage[:], ltmp[:], ALU.add)
                if qc == 3:
                    p.ts(ltmp[:, 0:511], vT[:, ct * 2048 + qc * 512 + 1: ct * 2048 + 2048],
                         lepe[:, ct * 4 + 2:ct * 4 + 3], ALU.mult)
                    p.tt(lstage[:, 0:511], lstage[:, 0:511], ltmp[:, 0:511], ALU.add)
                else:
                    p.ts(ltmp[:], vT[:, ct * 2048 + qc * 512 + 1: ct * 2048 + qc * 512 + 513],
                         lepe[:, ct * 4 + 2:ct * 4 + 3], ALU.mult)
                    p.tt(lstage[:], lstage[:], ltmp[:], ALU.add)
                p.tt(osl, osl, lstage[:], ALU.add)
            for sub in range(4):
                t0 = base + qc * 512 + sub * 128
                for ct in range(8):
                    pt = psT[ct % 2][:, 0:128]
                    p.trans(pt, outT[:, ct * 2048 + qc * 512 + sub * 128:
                                     ct * 2048 + qc * 512 + sub * 128 + 128], identb[:])
                    p.copy(otile[:, ct * 128:(ct + 1) * 128], pt)
                p.dma_in(xstage[:], xs[t0:t0 + 128, :])
                p.tt(otile[:], otile[:], xstage[:], ALU.add)
                p.dma_out(ys[t0:t0 + 128, :], otile[:])
    p.finalize()

    def run(x, inp):
        w = np.asarray(inp["L0_attn_b_w_qkv"])
        bq = np.asarray(inp["L0_attn_b_b_qkv"])
        wl = np.asarray(inp["L0_attn_b_w_lepe"])
        bl = np.asarray(inp["L0_attn_b_b_lepe"])
        wT = np.ascontiguousarray(w.T)
        waug = np.zeros((1024, 3088), np.float32)
        waug[:, 0:2048] = wT[:, 0:2048]
        vbias = np.zeros((1040,), np.float32)
        for h in range(16):
            waug[:, 2048 + 65 * h:2048 + 65 * h + 64] = wT[:, 2048 + 64 * h:2048 + 64 * h + 64]
            vbias[65 * h:65 * h + 64] = bq[2048 + 64 * h:2048 + 64 * h + 64]
            vbias[65 * h + 64] = 1.0
        qkb_h = np.tile(bq[None, 0:2048], (128, 1)).astype(BFNP)
        vb_h = np.tile(vbias[None, :], (128, 1)).astype(BFNP)
        pos = np.arange(2048, dtype=np.float32)[:, None]
        div = np.exp(np.arange(0, 64, 2, dtype=np.float32) * (-np.log(10000.0) / 64))
        ang = pos * div[None, :]
        cosE = np.tile(np.cos(ang), (1, 8)).astype(np.float32)
        sinE = np.tile(np.sin(ang), (1, 8)).astype(np.float32)
        lep = np.concatenate([wl[:, 0, :], bl[:, None]], axis=1)
        lep = np.ascontiguousarray(
            lep.reshape(8, 128, 4).transpose(1, 0, 2).reshape(128, 32)).astype(np.float32)
        in_maps = [{"xs": x[c * TPC:(c + 1) * TPC], "wqkvT": waug, "qkb": qkb_h,
                    "vb": vb_h, "cosE": cosE, "sinE": sinE, "lepe": lep,
                    "identb": IDENTB} for c in range(NCORES)]
        res = run_bass_kernel_spmd(nc, in_maps, core_ids=list(range(NCORES)))
        return np.concatenate([np.asarray(r["ys"]) for r in res.results], axis=0)

    return run


def build_mla():
    p = Prog("mla")
    nc = p.nc
    xs = nc.dram_tensor("xs", [TPC, D], F32, kind="ExternalInput").ap()
    wqd = nc.dram_tensor("wq", [1024, 1024], F32, kind="ExternalInput").ap()
    wkvdd = nc.dram_tensor("wkvd", [1024, 256], F32, kind="ExternalInput").ap()
    wkupd = nc.dram_tensor("wkup", [256, 1024], F32, kind="ExternalInput").ap()
    wvupd = nc.dram_tensor("wvup", [256, 1040], F32, kind="ExternalInput").ap()
    woutd = nc.dram_tensor("wout", [1024, 1024], F32, kind="ExternalInput").ap()
    vonesd = nc.dram_tensor("vones", [128, 520], BF16, kind="ExternalInput").ap()
    maskbd = nc.dram_tensor("maskb", [128, 896], BF16, kind="ExternalInput").ap()
    identbd = nc.dram_tensor("identb", [128, 128], BF16, kind="ExternalInput").ap()
    ys = nc.dram_tensor("ys", [TPC, D], F32, kind="ExternalOutput").ap()

    xstage = p.sb("xstage", 128, 1024, F32)
    xc = p.sb("xc", 128, 1024, F32)
    sq = p.sb("sq", 128, 1024, F32)
    st = p.sb("st", 128, 8, F32)
    xn = p.sb("xn", 128, 1024, BF16)
    xnTt = p.sb("xnTt", 128, 8 * 128, BF16)
    q_sb = p.sb("q_sb", 128, 16 * 512, BF16)
    k_sb = p.sb("k_sb", 128, 16 * 512, BF16)
    v_sb = p.sb("v_sb", 128, 16 * 520, BF16)
    latT = p.sb("latT", 128, 2 * 2048, BF16)
    lstg = p.sb("lstg", 128, 256, BF16)
    outT = p.sb("outT", 128, 8 * 2048, BF16)
    pT = [p.sb("pT0", 128, 512, BF16), p.sb("pT1", 128, 512, BF16)]
    qTh = p.sb("qTh", 64, 2048, BF16)
    kTh = p.sb("kTh", 64, 2048, BF16)
    wq_sb = p.sb("wq_sb", 128, 8 * 512, BF16)
    wkvd_sb = p.sb("wkvd_sb", 128, 8 * 256, BF16)
    wkup_sb = p.sb("wkup_sb", 128, 2 * 512, BF16)
    wvup_sb = p.sb("wvup_sb", 128, 2 * 520, BF16)
    wout_sb = p.sb("wout_sb", 128, 8 * 1024, BF16)
    vones = p.sb("vones_s", 128, 520, BF16)
    maskb = p.sb("maskb_s", 128, 896, BF16)
    wstage = p.sb("wstage", 128, 1024, F32)
    rd_sb = p.sb("rd_sb", 1, 512, F32)
    rdb = p.sb("rdb", 1, 512, BF16)
    onesb = p.sb("onesb", 1, 64, BF16)
    bc_sb = p.sb("bc_sb", 64, 512, BF16)
    otile = p.sb("otile", 128, 1024, F32)
    identb = p.sb("identb_s", 128, 128, BF16)

    ps_a = p.ps("ps_a")
    ps_b = p.ps("ps_b")
    ps_s = [p.ps("ps_s0"), p.ps("ps_s1")]
    ps_o = p.ps("ps_o")
    ps_bc = p.ps("ps_bc")
    psT = [p.ps("psT0", 1024, BF16), p.ps("psT1", 1024, BF16)]

    p.dma_in(identb[:], identbd)
    p.dma_in(vones[:], vonesd)
    p.dma_in(maskb[:], maskbd)
    p.memset(onesb[:], 1.0)
    for kt in range(8):
        p.dma_in(wstage[:, 0:256], wkvdd[kt * 128:(kt + 1) * 128, :])
        p.copy(wkvd_sb[:, kt * 256:(kt + 1) * 256], wstage[:, 0:256])
    for kt in range(8):
        p.dma_in(wstage[:, 0:1024], woutd[kt * 128:(kt + 1) * 128, :])
        p.copy(wout_sb[:, kt * 1024:(kt + 1) * 1024], wstage[:, 0:1024])

    for b in range(BPC):
        base = b * S
        for half in range(2):
            hoff = half * 512
            for kt in range(8):
                p.dma_in(wstage[:, 0:512], wqd[kt * 128:(kt + 1) * 128, hoff:hoff + 512])
                p.copy(wq_sb[:, kt * 512:(kt + 1) * 512], wstage[:, 0:512])
            for kt in range(2):
                p.dma_in(wstage[:, 0:512], wkupd[kt * 128:(kt + 1) * 128, hoff:hoff + 512])
                p.copy(wkup_sb[:, kt * 512:(kt + 1) * 512], wstage[:, 0:512])
                p.dma_in(wstage[:, 0:520],
                         wvupd[kt * 128:(kt + 1) * 128, 520 * half:520 * half + 520])
                p.copy(wvup_sb[:, kt * 520:(kt + 1) * 520], wstage[:, 0:520])
            for tile in range(16):
                t0 = base + tile * 128
                p.dma_in(xstage[:], xs[t0:t0 + 128, :])
                ln_into(p, xstage[:], xc[:], sq[:], st[:], xn[:])
                for kt in range(8):
                    pt = psT[kt % 2][:, 0:128]
                    p.trans(pt, xn[:, kt * 128:(kt + 1) * 128], identb[:])
                    p.copy(xnTt[:, kt * 128:(kt + 1) * 128], pt)
                if half == 0:
                    for kt in range(8):
                        p.mm(ps_a[:, 0:256], xnTt[:, kt * 128:(kt + 1) * 128],
                             wkvd_sb[:, kt * 256:(kt + 1) * 256],
                             start=(kt == 0), stop=(kt == 7))
                    p.copy(lstg[:], ps_a[:, 0:256])
                    for kt2 in range(2):
                        pt = psT[kt2 % 2][:, 128:256]
                        p.trans(pt, lstg[:, kt2 * 128:(kt2 + 1) * 128], identb[:])
                        p.copy(latT[:, kt2 * 2048 + tile * 128:kt2 * 2048 + tile * 128 + 128], pt)
                for kt in range(8):
                    p.mm(ps_b[:, :], xnTt[:, kt * 128:(kt + 1) * 128],
                         wq_sb[:, kt * 512:(kt + 1) * 512], start=(kt == 0), stop=(kt == 7))
                p.copy(q_sb[:, tile * 512:(tile + 1) * 512], ps_b[:, :])
                for kt2 in range(2):
                    p.mm(ps_a[:, :], latT[:, kt2 * 2048 + tile * 128:kt2 * 2048 + tile * 128 + 128],
                         wkup_sb[:, kt2 * 512:(kt2 + 1) * 512], start=(kt2 == 0), stop=(kt2 == 1))
                p.copy(k_sb[:, tile * 512:(tile + 1) * 512], ps_a[:, :])
                for kt2 in range(2):
                    p.mm(ps_b[:, 0:512], latT[:, kt2 * 2048 + tile * 128:kt2 * 2048 + tile * 128 + 128],
                         wvup_sb[:, kt2 * 520:kt2 * 520 + 512], start=(kt2 == 0), stop=(kt2 == 1))
                for kt2 in range(2):
                    p.mm(ps_s[0][:, 0:8], latT[:, kt2 * 2048 + tile * 128:kt2 * 2048 + tile * 128 + 128],
                         wvup_sb[:, kt2 * 520 + 512:kt2 * 520 + 520], start=(kt2 == 0), stop=(kt2 == 1))
                p.tt(v_sb[:, tile * 520:tile * 520 + 512], ps_b[:, 0:512],
                     vones[:, 0:512], ALU.add)
                p.tt(v_sb[:, tile * 520 + 512:tile * 520 + 520], ps_s[0][:, 0:8],
                     vones[:, 512:520], ALU.add)
            for h in range(8):
                hh = half * 8 + h
                for tile in range(16):
                    pt = psT[tile % 2][:, 256:384]
                    p.trans(pt[0:64, :], q_sb[:, tile * 512 + h * 64:tile * 512 + h * 64 + 64],
                            identb[:])
                    p.copy(qTh[:, tile * 128:(tile + 1) * 128], pt[0:64, :])
                    pt2 = psT[tile % 2][:, 384:512]
                    p.trans(pt2[0:64, :], k_sb[:, tile * 512 + h * 64:tile * 512 + h * 64 + 64],
                            identb[:])
                    p.copy(kTh[:, tile * 128:(tile + 1) * 128], pt2[0:64, :])
                for qc in range(4):
                    ktmax = 4 * (qc + 1)
                    for kt in range(ktmax):
                        pss = ps_s[kt % 2]
                        p.mm(pss[:, :], kTh[:, kt * 128:(kt + 1) * 128],
                             qTh[:, qc * 512:(qc + 1) * 512], start=True, stop=True)
                        if kt >= 4 * qc:
                            s0 = 384 + 512 * qc - 128 * kt
                            p.tt(pss[:, :], pss[:, :], maskb[:, s0:s0 + 512], ALU.add)
                        p.act(pT[kt % 2][:, :], pss[:, :], AF.Exp, scale=0.125)
                        p.mm(ps_o[0:65, :], v_sb[:, kt * 520 + h * 65:kt * 520 + h * 65 + 65],
                             pT[kt % 2][:, :], start=(kt == 0), stop=(kt == ktmax - 1))
                    p.recip(rd_sb[0:1, :], ps_o[64:65, :])
                    p.copy(rdb[0:1, :], rd_sb[0:1, :])
                    p.mm(ps_bc[0:64, :], onesb[:, :], rdb[0:1, :], start=True, stop=True)
                    p.copy(bc_sb[:, :], ps_bc[0:64, :])
                    p.tt(outT[64 * (hh % 2):64 * (hh % 2) + 64,
                              (hh // 2) * 2048 + qc * 512:(hh // 2) * 2048 + qc * 512 + 512],
                         ps_o[0:64, :], bc_sb[:, :], ALU.mult)
        for qc in range(4):
            for sub in range(4):
                t0 = base + qc * 512 + sub * 128
                for nch in range(2):
                    for kt in range(8):
                        p.mm(ps_a[:, :],
                             outT[:, kt * 2048 + qc * 512 + sub * 128:
                                  kt * 2048 + qc * 512 + sub * 128 + 128],
                             wout_sb[:, kt * 1024 + nch * 512:kt * 1024 + nch * 512 + 512],
                             start=(kt == 0), stop=(kt == 7))
                    p.copy(otile[:, nch * 512:(nch + 1) * 512], ps_a[:, :])
                p.dma_in(xstage[:], xs[t0:t0 + 128, :])
                p.tt(otile[:], otile[:], xstage[:], ALU.add)
                p.dma_out(ys[t0:t0 + 128, :], otile[:])
    p.finalize()

    def run(x, inp):
        wq = np.ascontiguousarray(np.asarray(inp["L1_attn_a_Wq"]).reshape(1024, 1024))
        wkvd = np.ascontiguousarray(np.asarray(inp["L1_attn_a_Wkv_down"]))
        wkup = np.ascontiguousarray(np.asarray(inp["L1_attn_a_Wk_up"]).reshape(256, 1024))
        wvup_r = np.asarray(inp["L1_attn_a_Wv_up"]).reshape(256, 1024)
        vones = np.zeros((1, 520), np.float32)
        wvup = np.zeros((256, 1040), np.float32)
        for h in range(16):
            half, hl = divmod(h, 8)
            wvup[:, 520 * half + 65 * hl:520 * half + 65 * hl + 64] = \
                wvup_r[:, 64 * h:64 * h + 64]
        for hl in range(8):
            vones[0, 65 * hl + 64] = 1.0
        vones_h = np.tile(vones, (128, 1)).astype(BFNP)
        wout = np.ascontiguousarray(np.asarray(inp["L1_attn_a_Wout"]).reshape(1024, 1024))
        kp = np.arange(128)[:, None]
        j = np.arange(896)[None, :]
        maskb = np.where(j >= kp + 384, 0.0, NEG).astype(BFNP)
        in_maps = [{"xs": x[c * TPC:(c + 1) * TPC], "wq": wq, "wkvd": wkvd,
                    "wkup": wkup, "wvup": wvup, "wout": wout, "vones": vones_h,
                    "maskb": maskb, "identb": IDENTB} for c in range(NCORES)]
        res = run_bass_kernel_spmd(nc, in_maps, core_ids=list(range(NCORES)))
        return np.concatenate([np.asarray(r["ys"]) for r in res.results], axis=0)

    return run



def build_wattn():
    W = 128
    p = Prog("wattn")
    nc = p.nc
    xs = nc.dram_tensor("xs", [TPC, D], F32, kind="ExternalInput").ap()
    wqkvd = nc.dram_tensor("wqkv", [1024, 3072], F32, kind="ExternalInput").ap()
    woutd = nc.dram_tensor("wout", [1024, 1024], F32, kind="ExternalInput").ap()
    boutd = nc.dram_tensor("bout", [128, 1024], F32, kind="ExternalInput").ap()
    maskwd = nc.dram_tensor("maskw", [128, 128], BF16, kind="ExternalInput").ap()
    identbd = nc.dram_tensor("identb", [128, 128], BF16, kind="ExternalInput").ap()
    ys = nc.dram_tensor("ys", [TPC, D], F32, kind="ExternalOutput").ap()

    xstage = p.sb("xstage", 128, 1024, F32)
    xc = p.sb("xc", 128, 1024, F32)
    sq = p.sb("sq", 128, 1024, F32)
    st = p.sb("st", 128, 8, F32)
    xn = p.sb("xn", 128, 1024, BF16)
    xnTt = p.sb("xnTt", 128, 8 * 128, BF16)
    q_sb = p.sb("q_sb", 128, 16 * 1024, BF16)
    k_sb = p.sb("k_sb", 128, 16 * 1025, BF16)
    zk = p.sb("zk", 128, 1025, BF16)
    zkT = p.sb("zkT", 128, 8 * 128, BF16)
    qT = p.sb("qT", 128, 8 * 128, BF16)
    kT = [p.sb("kT0", 128, 8 * 128, BF16), p.sb("kT1", 128, 8 * 128, BF16)]
    pTa = p.sb("pTa", 128, 128, BF16)
    pTb = p.sb("pTb", 128, 128, BF16)
    outTw = p.sb("outTw", 128, 8 * 128, BF16)
    wqkv_sb = p.sb("wqkv_sb", 128, 8 * 2048, BF16)
    wout_sb = p.sb("wout_sb", 128, 8 * 1024, BF16)
    bout = p.sb("bout_s", 128, 1024, F32)
    maskw = p.sb("maskw_s", 128, 128, BF16)
    wstage = p.sb("wstage", 128, 2048, F32)
    rd_sb = p.sb("rd_sb", 1, 128, F32)
    rdb = p.sb("rdb", 1, 128, BF16)
    onesb = p.sb("onesb", 1, 128, BF16)
    bc_sb = p.sb("bc_sb", 128, 128, BF16)
    otile = p.sb("otile", 128, 1024, F32)
    identb = p.sb("identb_s", 128, 128, BF16)

    ps_a = p.ps("ps_a")
    ps_b = p.ps("ps_b")
    ps_s = [p.ps("ps_s0"), p.ps("ps_s1")]
    ps_o = p.ps("ps_o")
    ps_dn = p.ps("ps_dn")
    ps_bc = p.ps("ps_bc")
    psT = p.ps("psT0", 1024, BF16)

    p.dma_in(identb[:], identbd)
    p.dma_in(maskw[:], maskwd)
    p.dma_in(bout[:], boutd)
    p.memset(onesb[:], 1.0)
    p.memset(zk[:], 0.0)
    p.memset(zk[:, 1024:1025], 1.0)
    p.memset(zkT[:], 0.0)
    for kt in range(8):
        p.dma_in(wstage[:, 0:2048], wqkvd[kt * 128:(kt + 1) * 128, 0:2048])
        p.copy(wqkv_sb[:, kt * 2048:(kt + 1) * 2048], wstage[:, 0:2048])
    for kt in range(8):
        p.dma_in(wstage[:, 0:1024], woutd[kt * 128:(kt + 1) * 128, :])
        p.copy(wout_sb[:, kt * 1024:(kt + 1) * 1024], wstage[:, 0:1024])

    for b in range(BPC):
        base = b * S
        for tile in range(16):
            t0 = base + tile * 128
            p.dma_in(xstage[:], xs[t0:t0 + 128, :])
            ln_into(p, xstage[:], xc[:], sq[:], st[:], xn[:])
            for kt in range(8):
                pt = psT[:, 0:128] if kt % 2 == 0 else psT[:, 128:256]
                p.trans(pt, xn[:, kt * 128:(kt + 1) * 128], identb[:])
                p.copy(xnTt[:, kt * 128:(kt + 1) * 128], pt)
            for nch in range(2):
                for kt in range(8):
                    p.mm(ps_a[:, :], xnTt[:, kt * 128:(kt + 1) * 128],
                         wqkv_sb[:, kt * 2048 + nch * 512:kt * 2048 + nch * 512 + 512],
                         start=(kt == 0), stop=(kt == 7))
                p.copy(q_sb[:, tile * 1024 + nch * 512:tile * 1024 + nch * 512 + 512],
                       ps_a[:, :])
                for kt in range(8):
                    p.mm(ps_b[:, :], xnTt[:, kt * 128:(kt + 1) * 128],
                         wqkv_sb[:, kt * 2048 + 1024 + nch * 512:
                                  kt * 2048 + 1024 + nch * 512 + 512],
                         start=(kt == 0), stop=(kt == 7))
                p.copy(k_sb[:, tile * 1025 + nch * 512:tile * 1025 + nch * 512 + 512],
                       ps_b[:, :])
            p.memset(k_sb[:, tile * 1025 + 1024:tile * 1025 + 1025], 1.0)
        for w in range(16):
            t0 = base + w * 128
            for kt in range(8):
                pt = psT[:, 256:384] if kt % 2 == 0 else psT[:, 384:512]
                p.trans(pt, q_sb[:, w * 1024 + kt * 128:w * 1024 + kt * 128 + 128], identb[:])
                p.copy(qT[:, kt * 128:(kt + 1) * 128], pt)
                pt2 = psT[:, 512:640] if kt % 2 == 0 else psT[:, 640:768]
                p.trans(pt2, k_sb[:, w * 1025 + kt * 128:w * 1025 + kt * 128 + 128], identb[:])
                p.copy(kT[w % 2][:, kt * 128:(kt + 1) * 128], pt2)
            kTprev = zkT if w == 0 else kT[(w - 1) % 2]
            kprev_cols = zk[:, 0:1025] if w == 0 else k_sb[:, (w - 1) * 1025:w * 1025]
            kcur_cols = k_sb[:, w * 1025:(w + 1) * 1025]
            for kt in range(8):
                p.mm(ps_s[0][:, 0:128], kTprev[:, kt * 128:(kt + 1) * 128],
                     qT[:, kt * 128:(kt + 1) * 128], start=(kt == 0), stop=(kt == 7))
            for kt in range(8):
                p.mm(ps_s[1][:, 0:128], kT[w % 2][:, kt * 128:(kt + 1) * 128],
                     qT[:, kt * 128:(kt + 1) * 128], start=(kt == 0), stop=(kt == 7))
            p.tt(ps_s[1][:, 0:128], ps_s[1][:, 0:128], maskw[:], ALU.add)
            p.act(pTa[:], ps_s[0][:, 0:128], AF.Exp, scale=1.0 / 32.0)
            p.act(pTb[:], ps_s[1][:, 0:128], AF.Exp, scale=1.0 / 32.0)
            p.mm(ps_dn[0:1, 0:128], kprev_cols[:, 1024:1025], pTa[:], start=True, stop=False)
            p.mm(ps_dn[0:1, 0:128], kcur_cols[:, 1024:1025], pTb[:], start=False, stop=True)
            p.recip(rd_sb[0:1, :], ps_dn[0:1, 0:128])
            p.copy(rdb[0:1, :], rd_sb[0:1, :])
            p.mm(ps_bc[:, 0:128], onesb[:, :], rdb[0:1, :], start=True, stop=True)
            p.copy(bc_sb[:, :], ps_bc[:, 0:128])
            for m in range(8):
                p.mm(ps_o[:, 0:128], kprev_cols[:, m * 128:(m + 1) * 128], pTa[:],
                     start=True, stop=False)
                p.mm(ps_o[:, 0:128], kcur_cols[:, m * 128:(m + 1) * 128], pTb[:],
                     start=False, stop=True)
                p.tt(outTw[:, m * 128:(m + 1) * 128], ps_o[:, 0:128], bc_sb[:], ALU.mult)
            for nch in range(2):
                for m in range(8):
                    p.mm(ps_a[:, :], outTw[:, m * 128:(m + 1) * 128],
                         wout_sb[:, m * 1024 + nch * 512:m * 1024 + nch * 512 + 512],
                         start=(m == 0), stop=(m == 7))
                p.tt(otile[:, nch * 512:(nch + 1) * 512], ps_a[:, :],
                     bout[:, nch * 512:(nch + 1) * 512], ALU.add)
            p.dma_in(xstage[:], xs[t0:t0 + 128, :])
            p.tt(otile[:], otile[:], xstage[:], ALU.add)
            p.dma_out(ys[t0:t0 + 128, :], otile[:])
    p.finalize()

    def run(x, inp):
        wqkv = np.ascontiguousarray(np.asarray(inp["L1_attn_b_w_qkv"]))
        wout = np.ascontiguousarray(np.asarray(inp["L1_attn_b_w_out"]))
        bout = np.tile(np.asarray(inp["L1_attn_b_b_out"])[None, :], (128, 1)).astype(np.float32)
        jj = np.arange(128)[None, :]
        kk = np.arange(128)[:, None]
        maskw = np.where(kk > jj, NEG, 0.0).astype(BFNP)
        in_maps = [{"xs": x[c * TPC:(c + 1) * TPC], "wqkv": wqkv, "wout": wout,
                    "bout": bout, "maskw": maskw, "identb": IDENTB}
                   for c in range(NCORES)]
        res = run_bass_kernel_spmd(nc, in_maps, core_ids=list(range(NCORES)))
        return np.concatenate([np.asarray(r["ys"]) for r in res.results], axis=0)

    return run



def kernel(**inp):
    x = np.ascontiguousarray(np.asarray(inp["L0_attn_a_x"], np.float32).reshape(T, D))
    mingru = get_prog("mingru", build_mingru)
    moe0 = get_prog("moe0", lambda: build_moe(0))
    moe1 = get_prog("moe1", lambda: build_moe(1))
    moesum = get_prog("moesum", build_moesum)
    attn0 = get_prog("attn0", build_attn0)
    mla = get_prog("mla", build_mla)
    wattn = get_prog("wattn", build_wattn)

    x = mingru(x, inp)
    parts = moe0(x, inp, "L0_ffn")
    x = moesum(x, parts)
    x = attn0(x, inp)
    parts = moe0(x, inp, "L0_ffn")
    x = moesum(x, parts)
    x = mla(x, inp)
    parts = moe1(x, inp, "L1_ffn")
    x = moesum(x, parts)
    x = wattn(x, inp)
    parts = moe1(x, inp, "L1_ffn")
    x = moesum(x, parts)
    return x.reshape(B, S, D)


# version 02 (final)
# speedup vs baseline: 1.0406x; 1.0406x over previous
import contextlib
import numpy as np
import ml_dtypes
import concourse.bass as bass
import concourse.mybir as mybir
from concourse.bass_utils import run_bass_kernel_spmd

F32 = mybir.dt.float32
BF16 = mybir.dt.bfloat16
AF = mybir.ActivationFunctionType
ALU = mybir.AluOpType
AX = mybir.AxisListType
BFNP = ml_dtypes.bfloat16

ENGINES = ("tensor", "vector", "scalar", "gpsimd", "sync")

B, S, D = 16, 2048, 1024
T = B * S
NCORES = 8
BPC = B // NCORES
TPC = BPC * S
NEG = -1.0e30


class Prog:
    def __init__(self, name="prog"):
        self.nc = bass.Bass("TRN2", target_bir_lowering=False)
        self.stack = contextlib.ExitStack()
        self.sem = {e: self.stack.enter_context(self.nc.semaphore(f"sem_{e}"))
                    for e in ENGINES}
        self.cnt = dict.fromkeys(ENGINES, 0)
        self.waited = {e: dict.fromkeys(ENGINES, 0) for e in ENGINES}
        self.lw = {}
        self.rd = {}
        self.out_sem_target = 0
        self._consts = {}
        self._const_sb = None
        self._const_n = 0

    def sb(self, name, part, free, dt):
        return self.stack.enter_context(self.nc.sbuf_tensor(name, [part, free], dt))

    def ps(self, name, free=512, dt=F32, part=128):
        return self.stack.enter_context(self.nc.psum_tensor(name, [part, free], dt))

    def const(self, value, dt=F32):
        key = (dt, float(value))
        if key in self._consts:
            return value
        if self._const_sb is None:
            self._const_sb = self.stack.enter_context(
                self.nc.sbuf_tensor("const_pool", [128, 64], F32))
        ap = self._const_sb[:, self._const_n:self._const_n + 1]
        self._const_n += 1
        self._emit("vector", self.nc.vector.memset, [ap], [], ap, float(value))
        self.nc.const_aps.register(dt, float(value), ap)
        self._consts[key] = ap
        return value

    @staticmethod
    def _key(ap):
        return ap.tensor.name

    def _wait(self, eng, dep_eng, val):
        if val > self.waited[eng][dep_eng]:
            getattr(self.nc, eng).wait_ge(self.sem[dep_eng], val)
            self.waited[eng][dep_eng] = val

    def _pre(self, eng, outs, ins, skip_waw_tensor=False):
        for ap in ins:
            w = self.lw.get(self._key(ap))
            if w:
                self._wait(eng, w[0], w[1])
        for ap in outs:
            k = self._key(ap)
            w = self.lw.get(k)
            if w and not (skip_waw_tensor and w[0] == "tensor" and eng == "tensor"):
                self._wait(eng, w[0], w[1])
            for e2, v in self.rd.get(k, {}).items():
                self._wait(eng, e2, v)

    def _post(self, eng, outs, ins, step):
        self.cnt[eng] += step
        c = self.cnt[eng]
        for ap in ins:
            self.rd.setdefault(self._key(ap), {})[eng] = c
        for ap in outs:
            k = self._key(ap)
            self.lw[k] = (eng, c)
            self.rd[k] = {}

    def _emit(self, eng, fn, outs, ins, *args, step=1, skip_waw_tensor=False, **kw):
        self._pre(eng, outs, ins, skip_waw_tensor)
        instr = fn(*args, **kw)
        instr.then_inc(self.sem[eng], step)
        self._post(eng, outs, ins, step)
        return instr

    def dma_in(self, dst, src, slow=False):
        kw = {"allow_slow_non_contiguous": True} if slow else {}
        self._emit("sync", self.nc.sync.dma_start, [dst], [src],
                   dst, src, step=16, **kw)

    def dma_out(self, dst, src):
        self._emit("gpsimd", self.nc.gpsimd.dma_start, [dst], [src],
                   dst, src, step=16)
        self.out_sem_target = self.cnt["gpsimd"]

    def mm(self, out, lhsT, rhs, start, stop):
        self._emit("tensor", self.nc.tensor.matmul, [out], [lhsT, rhs],
                   out, lhsT, rhs, start=start, stop=stop,
                   skip_waw_tensor=(not start))

    def trans(self, out, in_, ident):
        self._emit("tensor", self.nc.tensor.transpose, [out], [in_, ident],
                   out, in_, ident)

    def v(self, meth, out, ins, *args, **kw):
        self._emit("vector", getattr(self.nc.vector, meth), [out], list(ins),
                   out, *args, **kw)

    def copy(self, out, in_, eng="vector"):
        if eng == "scalar":
            self._emit("scalar", self.nc.scalar.copy, [out], [in_], out, in_)
        else:
            self.v("tensor_copy", out, [in_], in_)

    def tt(self, out, a, b, op):
        self.v("tensor_tensor", out, [a, b], a, b, op=op)

    def ts(self, out, in_, s1, op0, s2=None, op1=None):
        ins = [in_] + [x for x in (s1, s2) if isinstance(x, bass.AP)]
        if op1 is None:
            self._emit("vector", self.nc.vector.tensor_scalar, [out], ins,
                       out, in_, s1, None, op0)
        else:
            self._emit("vector", self.nc.vector.tensor_scalar, [out], ins,
                       out, in_, s1, s2, op0, op1)

    def act(self, out, in_, func, bias=None, scale=None, accum_out=None):
        kw = {}
        cins = []
        if bias is not None:
            if not isinstance(bias, bass.AP):
                bias = self.const(bias)
                cins.append(self._consts[(F32, float(bias))])
            kw["bias"] = bias
        if scale is not None:
            if not isinstance(scale, bass.AP):
                scale = self.const(scale)
                cins.append(self._consts[(F32, float(scale))])
            kw["scale"] = scale
        if accum_out is not None:
            kw["accum_out"] = accum_out
        ins = [in_] + ([bias] if isinstance(bias, bass.AP) else []) + cins
        outs = [out] + ([accum_out] if accum_out is not None else [])
        self._pre("scalar", outs, ins)
        instr = self.nc.scalar.activation(out, in_, func, **kw)
        instr.then_inc(self.sem["scalar"], 1)
        self._post("scalar", outs, ins, 1)

    def smul(self, out, in_, c):
        self._emit("scalar", self.nc.scalar.mul, [out], [in_], out, in_, c)

    def red(self, out, in_, op, negate=False):
        self._emit("vector", self.nc.vector.tensor_reduce, [out], [in_],
                   out, in_, axis=AX.X, op=op, negate=negate)

    def scan(self, out, a, b, initial=0.0):
        ins = [a, b] + ([initial] if isinstance(initial, bass.AP) else [])
        self._emit("vector", self.nc.vector.tensor_tensor_scan, [out], ins,
                   out, a, b, initial, ALU.mult, ALU.add)

    def memset(self, ap, val):
        self._emit("vector", self.nc.vector.memset, [ap], [], ap, float(val))

    def recip(self, out, in_):
        self.v("reciprocal", out, [in_], in_)

    def finalize(self):
        if self.out_sem_target:
            self.nc.vector.wait_ge(self.sem["gpsimd"], self.out_sem_target)
        if self.cnt["sync"]:
            self.nc.vector.wait_ge(self.sem["sync"], self.cnt["sync"])


def ln_into(p, xstage, xc, sq, st, xn_out):
    p.red(st[:, 0:1], xstage, ALU.add)
    p.smul(st[:, 1:2], st[:, 0:1], 1.0 / 1024)
    p.ts(xc, xstage, st[:, 1:2], ALU.subtract)
    p.act(sq, xc, AF.Square, accum_out=st[:, 2:3])
    p.act(st[:, 3:4], st[:, 2:3], AF.Sqrt, bias=1e-5, scale=1.0 / 1024)
    p.recip(st[:, 4:5], st[:, 3:4])
    p.ts(xn_out, xc, st[:, 4:5], ALU.mult)



TRACE = False


def _launch(name, nc, in_maps):
    if TRACE:
        res = run_bass_kernel_spmd(nc, in_maps, core_ids=list(range(NCORES)),
                                   trace=True)
        try:
            print(f"[trace] {name}: exec_time_ns={res.exec_time_ns}")
        except Exception as e:
            print(f"[trace] {name}: no exec_time ({e}); attrs={[a for a in dir(res) if 'time' in a.lower() or 'exec' in a.lower()]}")
        return res
    return run_bass_kernel_spmd(nc, in_maps, core_ids=list(range(NCORES)))


_PROGS = {}


def get_prog(name, builder):
    if name not in _PROGS:
        _PROGS[name] = builder()
    return _PROGS[name]


IDENTF = np.eye(128, dtype=np.float32)
IDENTB = IDENTF.astype(BFNP)



def build_mingru():
    p = Prog("mingru")
    nc = p.nc
    xs = nc.dram_tensor("xs", [TPC, D], F32, kind="ExternalInput").ap()
    whgT = nc.dram_tensor("whgT", [1024, 3072], F32, kind="ExternalInput").ap()
    woutT = nc.dram_tensor("woutT", [1536, 1024], F32, kind="ExternalInput").ap()
    identbd = nc.dram_tensor("identb", [128, 128], BF16, kind="ExternalInput").ap()
    ys = nc.dram_tensor("ys", [TPC, D], F32, kind="ExternalOutput").ap()

    xstage = p.sb("xstage", 128, 2048, F32)
    xc = p.sb("xc", 128, 1024, F32)
    sq = p.sb("sq", 128, 1024, F32)
    st = p.sb("st", 128, 8, F32)
    xn = p.sb("xn", 128, 1024, BF16)
    xnT = p.sb("xnT", 128, 8 * 2048, BF16)
    identb = p.sb("identb_s", 128, 128, BF16)
    wstage = p.sb("wstage", 128, 2048, F32)
    whg_h = p.sb("whg_h", 128, 8 * 128, BF16)
    whg_g = p.sb("whg_g", 128, 8 * 128, BF16)
    wout_sb = p.sb("wout_sb", 128, 12 * 1024, BF16)
    a_sb = p.sb("a_sb", 128, 2048, BF16)
    b_sb = p.sb("b_sb", 128, 2048, BF16)
    hf = p.sb("hf", 128, 2048, BF16)
    hb = p.sb("hb", 128, 2048, BF16)
    hsum = p.sb("hsum", 128, 12 * 2048, BF16)
    t0f = p.sb("t0f", 128, 512, F32)
    t1f = p.sb("t1f", 128, 512, F32)
    t2f = p.sb("t2f", 128, 512, F32)
    t3f = p.sb("t3f", 128, 512, F32)
    oevac = p.sb("oevac", 128, 2 * 512, BF16)
    otile = p.sb("otile", 128, 4 * 1024, F32)

    ps_h = p.ps("ps_h")
    ps_g = p.ps("ps_g")
    psT = [p.ps("psT0", 1024, BF16), p.ps("psT1", 1024, BF16)]
    ps_o = [p.ps("ps_o0"), p.ps("ps_o1")]

    p.dma_in(identb[:], identbd)
    for kt in range(12):
        p.dma_in(wstage[:, 0:1024], woutT[kt * 128:(kt + 1) * 128, :])
        p.copy(wout_sb[:, kt * 1024:(kt + 1) * 1024], wstage[:, 0:1024])

    for b in range(BPC):
        base = b * S
        for tile in range(16):
            t0 = base + tile * 128
            xsl = xstage[:, (tile % 2) * 1024:(tile % 2) * 1024 + 1024]
            p.dma_in(xsl, xs[t0:t0 + 128, :])
            ln_into(p, xsl, xc[:], sq[:], st[:], xn[:, :])
            for kt in range(8):
                pt = psT[kt % 2][:, (kt % 2) * 0:128]
                p.trans(pt, xn[:, kt * 128:(kt + 1) * 128], identb[:])
                p.copy(xnT[:, kt * 2048 + tile * 128: kt * 2048 + tile * 128 + 128], pt)
        for ft in range(12):
            for kt in range(8):
                p.dma_in(wstage[:, 0:128], whgT[kt * 128:(kt + 1) * 128,
                                                ft * 128:(ft + 1) * 128])
                p.copy(whg_h[:, kt * 128:(kt + 1) * 128], wstage[:, 0:128])
                p.dma_in(wstage[:, 128:256], whgT[kt * 128:(kt + 1) * 128,
                                                  1536 + ft * 128:1536 + ft * 128 + 128])
                p.copy(whg_g[:, kt * 128:(kt + 1) * 128], wstage[:, 128:256])
            for tn in range(4):
                for kt in range(8):
                    p.mm(ps_h[:, :], whg_h[:, kt * 128:(kt + 1) * 128],
                         xnT[:, kt * 2048 + tn * 512: kt * 2048 + tn * 512 + 512],
                         start=(kt == 0), stop=(kt == 7))
                for kt in range(8):
                    p.mm(ps_g[:, :], whg_g[:, kt * 128:(kt + 1) * 128],
                         xnT[:, kt * 2048 + tn * 512: kt * 2048 + tn * 512 + 512],
                         start=(kt == 0), stop=(kt == 7))
                sl = slice(tn * 512, tn * 512 + 512)
                p.act(a_sb[:, sl], ps_g[:, :], AF.Sigmoid, scale=-1.0)
                p.act(t0f[:, :], ps_g[:, :], AF.Sigmoid)
                p.ts(t1f[:, :], ps_h[:, :], 0.0, ALU.min)
                p.act(t2f[:, :], t1f[:, :], AF.Sigmoid)
                p.ts(t3f[:, :], ps_h[:, :], 0.0, ALU.max)
                p.tt(t2f[:, :], t2f[:, :], t3f[:, :], ALU.add)
                p.tt(b_sb[:, sl], t0f[:, :], t2f[:, :], ALU.mult)
            p.scan(hf[:, :], a_sb[:, :], b_sb[:, :], 0.0)
            p.scan(hb[:, :], a_sb[:, ::-1], b_sb[:, ::-1], 0.0)
            p.tt(hsum[:, ft * 2048:(ft + 1) * 2048], hf[:, :], hb[:, :], ALU.add)
        for tn in range(4):
            for dt in range(8):
                po = ps_o[dt % 2]
                for kt in range(12):
                    p.mm(po[:, :], wout_sb[:, kt * 1024 + dt * 128: kt * 1024 + dt * 128 + 128],
                         hsum[:, kt * 2048 + tn * 512: kt * 2048 + tn * 512 + 512],
                         start=(kt == 0), stop=(kt == 11))
                ev = oevac[:, (dt % 2) * 512:(dt % 2) * 512 + 512]
                p.copy(ev, po[:, :])
                for sub in range(4):
                    pt = psT[sub % 2][:, 128:256]
                    p.trans(pt, ev[:, sub * 128:sub * 128 + 128], identb[:])
                    p.copy(otile[:, sub * 1024 + dt * 128: sub * 1024 + dt * 128 + 128], pt)
            for sub in range(4):
                t0 = base + tn * 512 + sub * 128
                xsl = xstage[:, (sub % 2) * 1024:(sub % 2) * 1024 + 1024]
                p.dma_in(xsl, xs[t0:t0 + 128, :])
                osl = otile[:, sub * 1024:(sub + 1) * 1024]
                p.tt(osl, osl, xsl, ALU.add)
                p.dma_out(ys[t0:t0 + 128, :], osl)
    p.finalize()

    def run(x, inp):
        whgT_h = np.ascontiguousarray(inp["L0_attn_a_W_hg"].T)
        woutT_h = np.ascontiguousarray(inp["L0_attn_a_W_out"].T)
        in_maps = [{"xs": x[c * TPC:(c + 1) * TPC], "whgT": whgT_h,
                    "woutT": woutT_h, "identb": IDENTB} for c in range(NCORES)]
        res = _launch("mingru", nc, in_maps)
        return np.concatenate([np.asarray(r["ys"]) for r in res.results], axis=0)

    return run



def build_moe(layer):
    NF = 2048 if layer == 0 else 4096
    NFT = NF // 128
    CAP = float(int(T * 1.25 / 8))
    p = Prog(f"moe{layer}")
    nc = p.nc
    xall = nc.dram_tensor("xall", [T, D], F32, kind="ExternalInput").ap()
    wrd = nc.dram_tensor("wr", [1024, 8], F32, kind="ExternalInput").ap()
    identfd = nc.dram_tensor("identf", [128, 128], F32, kind="ExternalInput").ap()
    identbd = nc.dram_tensor("identb", [128, 128], BF16, kind="ExternalInput").ap()
    eohd = nc.dram_tensor("eoh", [128, 8], F32, kind="ExternalInput").ap()
    wgd = nc.dram_tensor("wg", [1024, NF], F32, kind="ExternalInput").ap()
    wdd = nc.dram_tensor("wd", [NF, 1024], F32, kind="ExternalInput").ap()
    if layer == 0:
        wud = nc.dram_tensor("wu", [1024, NF], F32, kind="ExternalInput").ap()
    else:
        b1d = nc.dram_tensor("b1", [128, NFT], F32, kind="ExternalInput").ap()
        b2d = nc.dram_tensor("b2", [128, 8], F32, kind="ExternalInput").ap()
        ltrid = nc.dram_tensor("ltri", [128, 128], F32, kind="ExternalInput").ap()
    part = nc.dram_tensor("part", [T, D], BF16, kind="ExternalOutput").ap()

    xstage = p.sb("xstage", 128, 1024, F32)
    xc = p.sb("xc", 128, 1024, F32)
    sq = p.sb("sq", 128, 1024, F32)
    st = p.sb("st", 128, 8, F32)
    xn32 = p.sb("xn32", 128, 1024, F32)
    xnT32t = p.sb("xnT32t", 128, 8 * 128, F32)
    xnTbs = [p.sb("xnTb0", 128, 8 * 512, BF16), p.sb("xnTb1", 128, 8 * 512, BF16)]
    identf = p.sb("identf_s", 128, 128, F32)
    identb = p.sb("identb_s", 128, 128, BF16)
    eoh = p.sb("eoh_s", 128, 8, F32)
    wr_sb = p.sb("wr_sb", 128, 64, F32)
    wstage = p.sb("wstage", 128, 1024, F32)
    wg_sb = p.sb("wg_sb", 128, 8 * NF, BF16)
    wd_sb = p.sb("wd_sb", 128, NFT * 1024, BF16)
    if layer == 0:
        wu_sb = p.sb("wu_sb", 128, 8 * NF, BF16)
        tsl = p.sb("tsl", 128, 512, F32)
    else:
        b1_sb = p.sb("b1_sb", 128, NFT, F32)
        b2_sb = p.sb("b2_sb", 128, 8, F32)
        ltri = p.sb("ltri_s", 128, 128, F32)
        onescol = p.sb("onescol", 128, 1, F32)
        onesrow = p.sb("onesrow", 1, 128, F32)
        carry = p.sb("carry", 1, 2, F32)
        ind = p.sb("ind", 128, 2, F32)
        keep = p.sb("keep", 128, 2, F32)
    f_sb = p.sb("f_sb", 128, NFT * 512, BF16)
    ex = p.sb("ex", 128, 8, F32)
    ex2 = p.sb("ex2", 128, 8, F32)
    mask1 = p.sb("mask1", 128, 8, F32)
    mask2 = p.sb("mask2", 128, 8, F32)
    rtmp = p.sb("rtmp", 128, 8, F32)
    nw1 = p.sb("nw1", 128, 1, F32)
    nw2 = p.sb("nw2", 128, 1, F32)
    wa = p.sb("wa", 128, 1, F32)
    wb = p.sb("wb", 128, 1, F32)
    wcols = [p.sb("wcol0", 128, 4, F32), p.sb("wcol1", 128, 4, F32)]
    dstage = p.sb("dstage", 128, 2 * 512, BF16)
    dtmp = p.sb("dtmp", 128, 2 * 128, BF16)

    ps_g = [p.ps("ps_g0"), p.ps("ps_g1")]
    ps_d = [p.ps("ps_d0"), p.ps("ps_d1")]
    ps_r = p.ps("ps_r", 512, F32)
    psT = [p.ps("psT0", 1024, BF16), p.ps("psT1", 1024, BF16)]
    if layer == 0:
        ps_u = p.ps("ps_u")
    else:
        ps_c = p.ps("ps_c", 512, F32)

    p.dma_in(identf[:], identfd)
    p.dma_in(identb[:], identbd)
    p.dma_in(eoh[:], eohd)
    for kt in range(8):
        p.dma_in(wr_sb[:, kt * 8:(kt + 1) * 8], wrd[kt * 128:(kt + 1) * 128, :])
    for kt in range(8):
        for pc in range(NF // 1024):
            p.dma_in(wstage[:, 0:1024], wgd[kt * 128:(kt + 1) * 128,
                                            pc * 1024:(pc + 1) * 1024])
            p.copy(wg_sb[:, kt * NF + pc * 1024:kt * NF + (pc + 1) * 1024],
                   wstage[:, 0:1024])
    if layer == 0:
        for kt in range(8):
            for pc in range(NF // 1024):
                p.dma_in(wstage[:, 0:1024], wud[kt * 128:(kt + 1) * 128,
                                                pc * 1024:(pc + 1) * 1024])
                p.copy(wu_sb[:, kt * NF + pc * 1024:kt * NF + (pc + 1) * 1024],
                       wstage[:, 0:1024])
    for kt in range(NFT):
        p.dma_in(wstage[:, 0:1024], wdd[kt * 128:(kt + 1) * 128, :])
        p.copy(wd_sb[:, kt * 1024:(kt + 1) * 1024], wstage[:, 0:1024])
    if layer == 1:
        p.dma_in(b1_sb[:], b1d)
        p.dma_in(b2_sb[:], b2d)
        p.dma_in(ltri[:], ltrid)
        p.memset(onescol[:], 1.0)
        p.memset(onesrow[:], 1.0)
        p.memset(carry[:], 0.0)

    for ch in range(64):
        c0 = ch * 512
        xnTb = xnTbs[ch % 2]
        wcol = wcols[ch % 2]
        for sub in range(4):
            t0 = c0 + sub * 128
            p.dma_in(xstage[:], xall[t0:t0 + 128, :])
            ln_into(p, xstage[:], xc[:], sq[:], st[:], xn32[:])
            for kt in range(8):
                p.trans(ps_r[:, 128:256], xn32[:, kt * 128:(kt + 1) * 128], identf[:])
                p.copy(xnT32t[:, kt * 128:(kt + 1) * 128], ps_r[:, 128:256])
                p.copy(xnTb[:, kt * 512 + sub * 128: kt * 512 + sub * 128 + 128],
                       xnT32t[:, kt * 128:(kt + 1) * 128])
            for kt in range(8):
                p.mm(ps_r[:, 0:8], xnT32t[:, kt * 128:(kt + 1) * 128],
                     wr_sb[:, kt * 8:(kt + 1) * 8], start=(kt == 0), stop=(kt == 7))
            p.red(st[:, 5:6], ps_r[:, 0:8], ALU.max, negate=True)
            p.act(ex[:], ps_r[:, 0:8], AF.Exp, bias=st[:, 5:6])
            p.red(st[:, 6:7], ex[:], ALU.add)
            p.red(st[:, 7:8], ex[:], ALU.max)
            p.ts(mask1[:], ex[:], st[:, 7:8], ALU.is_ge)
            p.ts(rtmp[:], mask1[:], 1e30, ALU.mult)
            p.tt(ex2[:], ex[:], rtmp[:], ALU.subtract)
            p.red(st[:, 0:1], ex2[:], ALU.max)
            p.ts(mask2[:], ex2[:], st[:, 0:1], ALU.is_ge)
            if layer == 0:
                p.tt(nw1[:], st[:, 7:8], st[:, 0:1], ALU.subtract)
                p.recip(nw2[:], st[:, 6:7])
                p.tt(nw1[:], nw1[:], nw2[:], ALU.mult)
                p.act(nw1[:], nw1[:], AF.Sigmoid)
                p.ts(nw2[:], nw1[:], -1.0, ALU.mult, 1.0, ALU.add)
            else:
                p.tt(nw1[:], st[:, 7:8], st[:, 0:1], ALU.add)
                p.recip(nw1[:], nw1[:])
                p.tt(nw2[:], st[:, 0:1], nw1[:], ALU.mult)
                p.tt(nw1[:], st[:, 7:8], nw1[:], ALU.mult)
            p.tt(rtmp[:], mask1[:], eoh[:], ALU.mult)
            p.red(wa[:], rtmp[:], ALU.add)
            p.tt(rtmp[:], mask2[:], eoh[:], ALU.mult)
            p.red(wb[:], rtmp[:], ALU.add)
            if layer == 0:
                p.tt(wa[:], wa[:], nw1[:], ALU.mult)
                p.tt(wb[:], wb[:], nw2[:], ALU.mult)
                p.tt(wcol[:, sub:sub + 1], wa[:], wb[:], ALU.add)
            else:
                p.copy(ind[:, 0:1], wa[:])
                p.copy(ind[:, 1:2], wb[:])
                p.mm(ps_c[:, 0:2], ltri[:], ind[:], start=True, stop=False)
                p.mm(ps_c[:, 0:2], onesrow[:], carry[:], start=False, stop=True)
                p.ts(keep[:], ps_c[:, 0:2], CAP, ALU.is_le)
                p.tt(wa[:], wa[:], nw1[:], ALU.mult)
                p.tt(wa[:], wa[:], keep[:, 0:1], ALU.mult)
                p.tt(wb[:], wb[:], nw2[:], ALU.mult)
                p.tt(wb[:], wb[:], keep[:, 1:2], ALU.mult)
                p.tt(wcol[:, sub:sub + 1], wa[:], wb[:], ALU.add)
                p.mm(ps_c[0:1, 4:6], onescol[:], ind[:], start=True, stop=True)
                p.tt(carry[:], carry[:], ps_c[0:1, 4:6], ALU.add)
        for ft in range(NFT):
            pg = ps_g[ft % 2]
            for kt in range(8):
                p.mm(pg[:, :], wg_sb[:, kt * NF + ft * 128: kt * NF + ft * 128 + 128],
                     xnTb[:, kt * 512:(kt + 1) * 512], start=(kt == 0), stop=(kt == 7))
            if layer == 0:
                for kt in range(8):
                    p.mm(ps_u[:, :], wu_sb[:, kt * NF + ft * 128: kt * NF + ft * 128 + 128],
                         xnTb[:, kt * 512:(kt + 1) * 512], start=(kt == 0), stop=(kt == 7))
                p.act(tsl[:], pg[:, :], AF.Silu)
                p.tt(f_sb[:, ft * 512:(ft + 1) * 512], tsl[:], ps_u[:, :], ALU.mult)
            else:
                p.act(f_sb[:, ft * 512:(ft + 1) * 512], pg[:, :], AF.Gelu,
                      bias=b1_sb[:, ft:ft + 1])
        for dt in range(8):
            po = ps_d[dt % 2]
            for kt in range(NFT):
                p.mm(po[:, :], wd_sb[:, kt * 1024 + dt * 128: kt * 1024 + dt * 128 + 128],
                     f_sb[:, kt * 512:(kt + 1) * 512], start=(kt == 0), stop=(kt == NFT - 1))
            dsl = dstage[:, (dt % 2) * 512:(dt % 2) * 512 + 512]
            if layer == 1:
                p.ts(dsl, po[:, :], b2_sb[:, dt:dt + 1], ALU.add)
            else:
                p.copy(dsl, po[:, :])
            for sub in range(4):
                pt = psT[sub % 2][:, 0:128]
                p.trans(pt, dsl[:, sub * 128:sub * 128 + 128], identb[:])
                dm = dtmp[:, (sub % 2) * 128:(sub % 2) * 128 + 128]
                p.copy(dm, pt)
                p.ts(dm, dm, wcol[:, sub:sub + 1], ALU.mult)
                p.dma_out(part[c0 + sub * 128:c0 + sub * 128 + 128,
                               dt * 128:(dt + 1) * 128], dm)
    p.finalize()

    def run(x, inp, pfx):
        wr_h = inp[f"{pfx}_router_kernel_DE" if layer == 0 else f"{pfx}_Wr"]
        in_maps = []
        for e in range(NCORES):
            eoh_h = np.zeros((128, 8), np.float32)
            eoh_h[:, e] = 1.0
            m = {"xall": x, "wr": np.ascontiguousarray(wr_h),
                 "identf": IDENTF, "identb": IDENTB, "eoh": eoh_h}
            if layer == 0:
                m["wg"] = np.ascontiguousarray(inp[f"{pfx}_kernel_gating_EDF"][e])
                m["wu"] = np.ascontiguousarray(inp[f"{pfx}_kernel_up_EDF"][e])
                m["wd"] = np.ascontiguousarray(inp[f"{pfx}_kernel_down_EFD"][e])
            else:
                m["wg"] = np.ascontiguousarray(inp[f"{pfx}_W1"][e])
                m["wd"] = np.ascontiguousarray(inp[f"{pfx}_W2"][e])
                m["b1"] = np.ascontiguousarray(
                    np.asarray(inp[f"{pfx}_b1"][e]).reshape(NFT, 128).T)
                m["b2"] = np.ascontiguousarray(
                    np.asarray(inp[f"{pfx}_b2"][e]).reshape(8, 128).T)
                m["ltri"] = np.ascontiguousarray(
                    np.tril(np.ones((128, 128), np.float32)).T)
            in_maps.append(m)
        res = _launch(f"moe{layer}", nc, in_maps)
        return [np.asarray(r["part"]) for r in res.results]

    return run



def build_moesum():
    p = Prog("moesum")
    nc = p.nc
    xs = nc.dram_tensor("xs", [TPC, D], F32, kind="ExternalInput").ap()
    pd = [nc.dram_tensor(f"p{e}", [TPC, D], BF16, kind="ExternalInput").ap()
          for e in range(8)]
    ys = nc.dram_tensor("ys", [TPC, D], F32, kind="ExternalOutput").ap()
    xstage = p.sb("xstage", 128, 2 * 1024, F32)
    pstage = p.sb("pstage", 128, 8 * 1024, BF16)
    acc = p.sb("acc", 128, 2 * 1024, F32)
    for tile in range(TPC // 128):
        t0 = tile * 128
        i = tile % 2
        xsl = xstage[:, i * 1024:(i + 1) * 1024]
        asl = acc[:, i * 1024:(i + 1) * 1024]
        p.dma_in(xsl, xs[t0:t0 + 128, :])
        for e in range(8):
            p.dma_in(pstage[:, e * 1024:(e + 1) * 1024], pd[e][t0:t0 + 128, :])
        p.tt(asl, xsl, pstage[:, 0:1024], ALU.add)
        for e in range(1, 8):
            p.tt(asl, asl, pstage[:, e * 1024:(e + 1) * 1024], ALU.add)
        p.dma_out(ys[t0:t0 + 128, :], asl)
    p.finalize()

    def run(x, parts):
        in_maps = []
        for c in range(NCORES):
            m = {"xs": x[c * TPC:(c + 1) * TPC]}
            for e in range(8):
                m[f"p{e}"] = parts[e][c * TPC:(c + 1) * TPC]
            in_maps.append(m)
        res = _launch("moesum", nc, in_maps)
        return np.concatenate([np.asarray(r["ys"]) for r in res.results], axis=0)

    return run


def build_attn0():
    p = Prog("attn0")
    nc = p.nc
    xs = nc.dram_tensor("xs", [TPC, D], F32, kind="ExternalInput").ap()
    wqkvT = nc.dram_tensor("wqkvT", [1024, 3088], F32, kind="ExternalInput").ap()
    qkbd = nc.dram_tensor("qkb", [128, 2048], BF16, kind="ExternalInput").ap()
    vbd = nc.dram_tensor("vb", [128, 1040], BF16, kind="ExternalInput").ap()
    cosd = nc.dram_tensor("cosE", [2048, 256], F32, kind="ExternalInput").ap()
    sind = nc.dram_tensor("sinE", [2048, 256], F32, kind="ExternalInput").ap()
    leped = nc.dram_tensor("lepe", [128, 32], F32, kind="ExternalInput").ap()
    identbd = nc.dram_tensor("identb", [128, 128], BF16, kind="ExternalInput").ap()
    ys = nc.dram_tensor("ys", [TPC, D], F32, kind="ExternalOutput").ap()

    xstage = p.sb("xstage", 128, 1024, F32)
    xc = p.sb("xc", 128, 1024, F32)
    sq = p.sb("sq", 128, 1024, F32)
    st = p.sb("st", 128, 8, F32)
    xn = p.sb("xn", 128, 1024, BF16)
    xnTt = p.sb("xnTt", 128, 8 * 128, BF16)
    q_sb = p.sb("q_sb", 128, 16 * 512, BF16)
    k_sb = p.sb("k_sb", 128, 16 * 512, BF16)
    v_sb = p.sb("v_sb", 128, 16 * 520, BF16)
    vT = p.sb("vT", 128, 8 * 2048, BF16)
    outT = p.sb("outT", 128, 8 * 2048, BF16)
    pT = [p.sb("pT0", 128, 512, BF16), p.sb("pT1", 128, 512, BF16)]
    qTh = p.sb("qTh", 64, 2048, BF16)
    kTh = p.sb("kTh", 64, 2048, BF16)
    ct_sb = p.sb("ct_sb", 128, 256, F32)
    st_sb = p.sb("st_sb", 128, 256, F32)
    qkb = p.sb("qkb_s", 128, 2048, BF16)
    vb = p.sb("vb_s", 128, 1040, BF16)
    lepe = p.sb("lepe_s", 128, 32, F32)
    wq_sb = p.sb("wq_sb", 128, 8 * 512, BF16)
    wk_sb = p.sb("wk_sb", 128, 8 * 512, BF16)
    wv_sb = p.sb("wv_sb", 128, 8 * 520, BF16)
    wstage = p.sb("wstage", 128, 1024, F32)
    stage = p.sb("stage", 128, 512, F32)
    tmpE = p.sb("tmpE", 128, 512, F32)
    tmpR = p.sb("tmpR", 128, 512, F32)
    tA = p.sb("tA", 128, 256, F32)
    tB = p.sb("tB", 128, 256, F32)
    lstage = p.sb("lstage", 128, 512, F32)
    ltmp = p.sb("ltmp", 128, 512, F32)
    rd_sb = p.sb("rd_sb", 1, 512, F32)
    rdb = p.sb("rdb", 1, 512, BF16)
    onesb = p.sb("onesb", 1, 64, BF16)
    bc_sb = p.sb("bc_sb", 64, 512, BF16)
    otile = p.sb("otile", 128, 1024, F32)
    identb = p.sb("identb_s", 128, 128, BF16)

    ps_a = p.ps("ps_a")
    ps_b = p.ps("ps_b")
    ps_s = [p.ps("ps_s0"), p.ps("ps_s1")]
    ps_o = p.ps("ps_o")
    ps_bc = p.ps("ps_bc")
    psT = [p.ps("psT0", 1024, BF16), p.ps("psT1", 1024, BF16)]

    p.dma_in(identb[:], identbd)
    p.dma_in(qkb[:], qkbd)
    p.dma_in(vb[:], vbd)
    p.dma_in(lepe[:], leped)
    p.memset(onesb[:], 1.0)

    def rope_block(dst_base, src):
        p.tt(tA[:], src[:, 0::2], ct_sb[:], ALU.mult)
        p.tt(tB[:], src[:, 1::2], st_sb[:], ALU.mult)
        p.tt(dst_base[:, 0::2], tA[:], tB[:], ALU.subtract)
        p.tt(tA[:], src[:, 0::2], st_sb[:], ALU.mult)
        p.tt(tB[:], src[:, 1::2], ct_sb[:], ALU.mult)
        p.tt(dst_base[:, 1::2], tA[:], tB[:], ALU.add)

    for b in range(BPC):
        base = b * S
        for half in range(2):
            hoff = half * 512
            for kt in range(8):
                r = slice(kt * 128, (kt + 1) * 128)
                p.dma_in(wstage[:, 0:512], wqkvT[r, hoff:hoff + 512])
                p.copy(wq_sb[:, kt * 512:(kt + 1) * 512], wstage[:, 0:512])
                p.dma_in(wstage[:, 0:512], wqkvT[r, 1024 + hoff:1024 + hoff + 512])
                p.copy(wk_sb[:, kt * 512:(kt + 1) * 512], wstage[:, 0:512])
                p.dma_in(wstage[:, 0:520], wqkvT[r, 2048 + 520 * half:2048 + 520 * half + 520])
                p.copy(wv_sb[:, kt * 520:(kt + 1) * 520], wstage[:, 0:520])
            for tile in range(16):
                t0 = base + tile * 128
                p.dma_in(xstage[:], xs[t0:t0 + 128, :])
                ln_into(p, xstage[:], xc[:], sq[:], st[:], xn[:])
                for kt in range(8):
                    pt = psT[kt % 2][:, 0:128]
                    p.trans(pt, xn[:, kt * 128:(kt + 1) * 128], identb[:])
                    p.copy(xnTt[:, kt * 128:(kt + 1) * 128], pt)
                p.dma_in(ct_sb[:], cosd[tile * 128:(tile + 1) * 128, :])
                p.dma_in(st_sb[:], sind[tile * 128:(tile + 1) * 128, :])
                for kt in range(8):
                    p.mm(ps_a[:, :], xnTt[:, kt * 128:(kt + 1) * 128],
                         wq_sb[:, kt * 512:(kt + 1) * 512], start=(kt == 0), stop=(kt == 7))
                p.tt(stage[:], ps_a[:, :], qkb[:, hoff:hoff + 512], ALU.add)
                p.ts(tmpE[:], stage[:], 0.0, ALU.min)
                p.act(tmpE[:], tmpE[:], AF.Exp)
                p.ts(tmpR[:], stage[:], 0.0, ALU.max)
                p.tt(stage[:], tmpE[:], tmpR[:], ALU.add)
                rope_block(q_sb[:, tile * 512:(tile + 1) * 512], stage)
                for kt in range(8):
                    p.mm(ps_b[:, :], xnTt[:, kt * 128:(kt + 1) * 128],
                         wk_sb[:, kt * 512:(kt + 1) * 512], start=(kt == 0), stop=(kt == 7))
                p.tt(stage[:], ps_b[:, :], qkb[:, 1024 + hoff:1024 + hoff + 512], ALU.add)
                p.ts(tmpE[:], stage[:], 0.0, ALU.min)
                p.act(tmpE[:], tmpE[:], AF.Exp)
                p.ts(tmpR[:], stage[:], 0.0, ALU.max)
                p.tt(stage[:], tmpE[:], tmpR[:], ALU.add)
                rope_block(k_sb[:, tile * 512:(tile + 1) * 512], stage)
                for kt in range(8):
                    p.mm(ps_a[:, 0:512], xnTt[:, kt * 128:(kt + 1) * 128],
                         wv_sb[:, kt * 520:kt * 520 + 512], start=(kt == 0), stop=(kt == 7))
                for kt in range(8):
                    p.mm(ps_b[:, 0:8], xnTt[:, kt * 128:(kt + 1) * 128],
                         wv_sb[:, kt * 520 + 512:kt * 520 + 520], start=(kt == 0), stop=(kt == 7))
                p.tt(v_sb[:, tile * 520:tile * 520 + 512], ps_a[:, 0:512],
                     vb[:, 520 * half:520 * half + 512], ALU.add)
                p.tt(v_sb[:, tile * 520 + 512:tile * 520 + 520], ps_b[:, 0:8],
                     vb[:, 520 * half + 512:520 * half + 520], ALU.add)
                for h in range(8):
                    hh = half * 8 + h
                    pt = psT[h % 2][:, 128:256]
                    p.trans(pt[0:64, :], v_sb[:, tile * 520 + h * 65:tile * 520 + h * 65 + 64],
                            identb[:])
                    p.copy(vT[64 * (hh % 2):64 * (hh % 2) + 64,
                             (hh // 2) * 2048 + tile * 128:(hh // 2) * 2048 + tile * 128 + 128],
                           pt[0:64, :])
            for h in range(8):
                hh = half * 8 + h
                for tile in range(16):
                    pt = psT[tile % 2][:, 256:384]
                    p.trans(pt[0:64, :], q_sb[:, tile * 512 + h * 64:tile * 512 + h * 64 + 64],
                            identb[:])
                    p.copy(qTh[:, tile * 128:(tile + 1) * 128], pt[0:64, :])
                    pt2 = psT[tile % 2][:, 384:512]
                    p.trans(pt2[0:64, :], k_sb[:, tile * 512 + h * 64:tile * 512 + h * 64 + 64],
                            identb[:])
                    p.copy(kTh[:, tile * 128:(tile + 1) * 128], pt2[0:64, :])
                for qc in range(4):
                    for kt in range(16):
                        pss = ps_s[kt % 2]
                        p.mm(pss[:, :], kTh[:, kt * 128:(kt + 1) * 128],
                             qTh[:, qc * 512:(qc + 1) * 512], start=True, stop=True)
                        p.act(pT[kt % 2][:, :], pss[:, :], AF.Exp, bias=-20.0, scale=0.125)
                        p.mm(ps_o[0:65, :], v_sb[:, kt * 520 + h * 65:kt * 520 + h * 65 + 65],
                             pT[kt % 2][:, :], start=(kt == 0), stop=(kt == 15))
                    p.recip(rd_sb[0:1, :], ps_o[64:65, :])
                    p.copy(rdb[0:1, :], rd_sb[0:1, :])
                    p.mm(ps_bc[0:64, :], onesb[:, :], rdb[0:1, :], start=True, stop=True)
                    p.copy(bc_sb[:, :], ps_bc[0:64, :])
                    p.tt(outT[64 * (hh % 2):64 * (hh % 2) + 64,
                              (hh // 2) * 2048 + qc * 512:(hh // 2) * 2048 + qc * 512 + 512],
                         ps_o[0:64, :], bc_sb[:, :], ALU.mult)
        for qc in range(4):
            for ct in range(8):
                vsl = vT[:, ct * 2048 + qc * 512: ct * 2048 + qc * 512 + 512]
                osl = outT[:, ct * 2048 + qc * 512: ct * 2048 + qc * 512 + 512]
                p.ts(lstage[:], vsl, lepe[:, ct * 4 + 1:ct * 4 + 2], ALU.mult,
                     lepe[:, ct * 4 + 3:ct * 4 + 4], ALU.add)
                if qc == 0:
                    p.ts(ltmp[:, 0:511], vT[:, ct * 2048:ct * 2048 + 511],
                         lepe[:, ct * 4 + 0:ct * 4 + 1], ALU.mult)
                    p.tt(lstage[:, 1:512], lstage[:, 1:512], ltmp[:, 0:511], ALU.add)
                else:
                    p.ts(ltmp[:], vT[:, ct * 2048 + qc * 512 - 1: ct * 2048 + qc * 512 + 511],
                         lepe[:, ct * 4 + 0:ct * 4 + 1], ALU.mult)
                    p.tt(lstage[:], lstage[:], ltmp[:], ALU.add)
                if qc == 3:
                    p.ts(ltmp[:, 0:511], vT[:, ct * 2048 + qc * 512 + 1: ct * 2048 + 2048],
                         lepe[:, ct * 4 + 2:ct * 4 + 3], ALU.mult)
                    p.tt(lstage[:, 0:511], lstage[:, 0:511], ltmp[:, 0:511], ALU.add)
                else:
                    p.ts(ltmp[:], vT[:, ct * 2048 + qc * 512 + 1: ct * 2048 + qc * 512 + 513],
                         lepe[:, ct * 4 + 2:ct * 4 + 3], ALU.mult)
                    p.tt(lstage[:], lstage[:], ltmp[:], ALU.add)
                p.tt(osl, osl, lstage[:], ALU.add)
            for sub in range(4):
                t0 = base + qc * 512 + sub * 128
                for ct in range(8):
                    pt = psT[ct % 2][:, 0:128]
                    p.trans(pt, outT[:, ct * 2048 + qc * 512 + sub * 128:
                                     ct * 2048 + qc * 512 + sub * 128 + 128], identb[:])
                    p.copy(otile[:, ct * 128:(ct + 1) * 128], pt)
                p.dma_in(xstage[:], xs[t0:t0 + 128, :])
                p.tt(otile[:], otile[:], xstage[:], ALU.add)
                p.dma_out(ys[t0:t0 + 128, :], otile[:])
    p.finalize()

    def run(x, inp):
        w = np.asarray(inp["L0_attn_b_w_qkv"])
        bq = np.asarray(inp["L0_attn_b_b_qkv"])
        wl = np.asarray(inp["L0_attn_b_w_lepe"])
        bl = np.asarray(inp["L0_attn_b_b_lepe"])
        wT = np.ascontiguousarray(w.T)
        waug = np.zeros((1024, 3088), np.float32)
        waug[:, 0:2048] = wT[:, 0:2048]
        vbias = np.zeros((1040,), np.float32)
        for h in range(16):
            waug[:, 2048 + 65 * h:2048 + 65 * h + 64] = wT[:, 2048 + 64 * h:2048 + 64 * h + 64]
            vbias[65 * h:65 * h + 64] = bq[2048 + 64 * h:2048 + 64 * h + 64]
            vbias[65 * h + 64] = 1.0
        qkb_h = np.tile(bq[None, 0:2048], (128, 1)).astype(BFNP)
        vb_h = np.tile(vbias[None, :], (128, 1)).astype(BFNP)
        pos = np.arange(2048, dtype=np.float32)[:, None]
        div = np.exp(np.arange(0, 64, 2, dtype=np.float32) * (-np.log(10000.0) / 64))
        ang = pos * div[None, :]
        cosE = np.tile(np.cos(ang), (1, 8)).astype(np.float32)
        sinE = np.tile(np.sin(ang), (1, 8)).astype(np.float32)
        lep = np.concatenate([wl[:, 0, :], bl[:, None]], axis=1)
        lep = np.ascontiguousarray(
            lep.reshape(8, 128, 4).transpose(1, 0, 2).reshape(128, 32)).astype(np.float32)
        in_maps = [{"xs": x[c * TPC:(c + 1) * TPC], "wqkvT": waug, "qkb": qkb_h,
                    "vb": vb_h, "cosE": cosE, "sinE": sinE, "lepe": lep,
                    "identb": IDENTB} for c in range(NCORES)]
        res = _launch("attn0", nc, in_maps)
        return np.concatenate([np.asarray(r["ys"]) for r in res.results], axis=0)

    return run


def build_mla():
    p = Prog("mla")
    nc = p.nc
    xs = nc.dram_tensor("xs", [TPC, D], F32, kind="ExternalInput").ap()
    wqd = nc.dram_tensor("wq", [1024, 1024], F32, kind="ExternalInput").ap()
    wkvdd = nc.dram_tensor("wkvd", [1024, 256], F32, kind="ExternalInput").ap()
    wkupd = nc.dram_tensor("wkup", [256, 1024], F32, kind="ExternalInput").ap()
    wvupd = nc.dram_tensor("wvup", [256, 1040], F32, kind="ExternalInput").ap()
    woutd = nc.dram_tensor("wout", [1024, 1024], F32, kind="ExternalInput").ap()
    vonesd = nc.dram_tensor("vones", [128, 520], BF16, kind="ExternalInput").ap()
    maskbd = nc.dram_tensor("maskb", [128, 896], BF16, kind="ExternalInput").ap()
    identbd = nc.dram_tensor("identb", [128, 128], BF16, kind="ExternalInput").ap()
    ys = nc.dram_tensor("ys", [TPC, D], F32, kind="ExternalOutput").ap()

    xstage = p.sb("xstage", 128, 1024, F32)
    xc = p.sb("xc", 128, 1024, F32)
    sq = p.sb("sq", 128, 1024, F32)
    st = p.sb("st", 128, 8, F32)
    xn = p.sb("xn", 128, 1024, BF16)
    xnTt = p.sb("xnTt", 128, 8 * 128, BF16)
    q_sb = p.sb("q_sb", 128, 16 * 512, BF16)
    k_sb = p.sb("k_sb", 128, 16 * 512, BF16)
    v_sb = p.sb("v_sb", 128, 16 * 520, BF16)
    latT = p.sb("latT", 128, 2 * 2048, BF16)
    lstg = p.sb("lstg", 128, 256, BF16)
    outT = p.sb("outT", 128, 8 * 2048, BF16)
    pT = [p.sb("pT0", 128, 512, BF16), p.sb("pT1", 128, 512, BF16)]
    qTh = p.sb("qTh", 64, 2048, BF16)
    kTh = p.sb("kTh", 64, 2048, BF16)
    wq_sb = p.sb("wq_sb", 128, 8 * 512, BF16)
    wkvd_sb = p.sb("wkvd_sb", 128, 8 * 256, BF16)
    wkup_sb = p.sb("wkup_sb", 128, 2 * 512, BF16)
    wvup_sb = p.sb("wvup_sb", 128, 2 * 520, BF16)
    wout_sb = p.sb("wout_sb", 128, 8 * 1024, BF16)
    vones = p.sb("vones_s", 128, 520, BF16)
    maskb = p.sb("maskb_s", 128, 896, BF16)
    wstage = p.sb("wstage", 128, 1024, F32)
    rd_sb = p.sb("rd_sb", 1, 512, F32)
    rdb = p.sb("rdb", 1, 512, BF16)
    onesb = p.sb("onesb", 1, 64, BF16)
    bc_sb = p.sb("bc_sb", 64, 512, BF16)
    otile = p.sb("otile", 128, 1024, F32)
    identb = p.sb("identb_s", 128, 128, BF16)

    ps_a = p.ps("ps_a")
    ps_b = p.ps("ps_b")
    ps_s = [p.ps("ps_s0"), p.ps("ps_s1")]
    ps_o = p.ps("ps_o")
    ps_bc = p.ps("ps_bc")
    psT = [p.ps("psT0", 1024, BF16), p.ps("psT1", 1024, BF16)]

    p.dma_in(identb[:], identbd)
    p.dma_in(vones[:], vonesd)
    p.dma_in(maskb[:], maskbd)
    p.memset(onesb[:], 1.0)
    for kt in range(8):
        p.dma_in(wstage[:, 0:256], wkvdd[kt * 128:(kt + 1) * 128, :])
        p.copy(wkvd_sb[:, kt * 256:(kt + 1) * 256], wstage[:, 0:256])
    for kt in range(8):
        p.dma_in(wstage[:, 0:1024], woutd[kt * 128:(kt + 1) * 128, :])
        p.copy(wout_sb[:, kt * 1024:(kt + 1) * 1024], wstage[:, 0:1024])

    for b in range(BPC):
        base = b * S
        for half in range(2):
            hoff = half * 512
            for kt in range(8):
                p.dma_in(wstage[:, 0:512], wqd[kt * 128:(kt + 1) * 128, hoff:hoff + 512])
                p.copy(wq_sb[:, kt * 512:(kt + 1) * 512], wstage[:, 0:512])
            for kt in range(2):
                p.dma_in(wstage[:, 0:512], wkupd[kt * 128:(kt + 1) * 128, hoff:hoff + 512])
                p.copy(wkup_sb[:, kt * 512:(kt + 1) * 512], wstage[:, 0:512])
                p.dma_in(wstage[:, 0:520],
                         wvupd[kt * 128:(kt + 1) * 128, 520 * half:520 * half + 520])
                p.copy(wvup_sb[:, kt * 520:(kt + 1) * 520], wstage[:, 0:520])
            for tile in range(16):
                t0 = base + tile * 128
                p.dma_in(xstage[:], xs[t0:t0 + 128, :])
                ln_into(p, xstage[:], xc[:], sq[:], st[:], xn[:])
                for kt in range(8):
                    pt = psT[kt % 2][:, 0:128]
                    p.trans(pt, xn[:, kt * 128:(kt + 1) * 128], identb[:])
                    p.copy(xnTt[:, kt * 128:(kt + 1) * 128], pt)
                if half == 0:
                    for kt in range(8):
                        p.mm(ps_a[:, 0:256], xnTt[:, kt * 128:(kt + 1) * 128],
                             wkvd_sb[:, kt * 256:(kt + 1) * 256],
                             start=(kt == 0), stop=(kt == 7))
                    p.copy(lstg[:], ps_a[:, 0:256])
                    for kt2 in range(2):
                        pt = psT[kt2 % 2][:, 128:256]
                        p.trans(pt, lstg[:, kt2 * 128:(kt2 + 1) * 128], identb[:])
                        p.copy(latT[:, kt2 * 2048 + tile * 128:kt2 * 2048 + tile * 128 + 128], pt)
                for kt in range(8):
                    p.mm(ps_b[:, :], xnTt[:, kt * 128:(kt + 1) * 128],
                         wq_sb[:, kt * 512:(kt + 1) * 512], start=(kt == 0), stop=(kt == 7))
                p.copy(q_sb[:, tile * 512:(tile + 1) * 512], ps_b[:, :])
                for kt2 in range(2):
                    p.mm(ps_a[:, :], latT[:, kt2 * 2048 + tile * 128:kt2 * 2048 + tile * 128 + 128],
                         wkup_sb[:, kt2 * 512:(kt2 + 1) * 512], start=(kt2 == 0), stop=(kt2 == 1))
                p.copy(k_sb[:, tile * 512:(tile + 1) * 512], ps_a[:, :])
                for kt2 in range(2):
                    p.mm(ps_b[:, 0:512], latT[:, kt2 * 2048 + tile * 128:kt2 * 2048 + tile * 128 + 128],
                         wvup_sb[:, kt2 * 520:kt2 * 520 + 512], start=(kt2 == 0), stop=(kt2 == 1))
                for kt2 in range(2):
                    p.mm(ps_s[0][:, 0:8], latT[:, kt2 * 2048 + tile * 128:kt2 * 2048 + tile * 128 + 128],
                         wvup_sb[:, kt2 * 520 + 512:kt2 * 520 + 520], start=(kt2 == 0), stop=(kt2 == 1))
                p.tt(v_sb[:, tile * 520:tile * 520 + 512], ps_b[:, 0:512],
                     vones[:, 0:512], ALU.add)
                p.tt(v_sb[:, tile * 520 + 512:tile * 520 + 520], ps_s[0][:, 0:8],
                     vones[:, 512:520], ALU.add)
            for h in range(8):
                hh = half * 8 + h
                for tile in range(16):
                    pt = psT[tile % 2][:, 256:384]
                    p.trans(pt[0:64, :], q_sb[:, tile * 512 + h * 64:tile * 512 + h * 64 + 64],
                            identb[:])
                    p.copy(qTh[:, tile * 128:(tile + 1) * 128], pt[0:64, :])
                    pt2 = psT[tile % 2][:, 384:512]
                    p.trans(pt2[0:64, :], k_sb[:, tile * 512 + h * 64:tile * 512 + h * 64 + 64],
                            identb[:])
                    p.copy(kTh[:, tile * 128:(tile + 1) * 128], pt2[0:64, :])
                for qc in range(4):
                    ktmax = 4 * (qc + 1)
                    for kt in range(ktmax):
                        pss = ps_s[kt % 2]
                        p.mm(pss[:, :], kTh[:, kt * 128:(kt + 1) * 128],
                             qTh[:, qc * 512:(qc + 1) * 512], start=True, stop=True)
                        if kt >= 4 * qc:
                            s0 = 384 + 512 * qc - 128 * kt
                            p.tt(pss[:, :], pss[:, :], maskb[:, s0:s0 + 512], ALU.add)
                        p.act(pT[kt % 2][:, :], pss[:, :], AF.Exp, scale=0.125)
                        p.mm(ps_o[0:65, :], v_sb[:, kt * 520 + h * 65:kt * 520 + h * 65 + 65],
                             pT[kt % 2][:, :], start=(kt == 0), stop=(kt == ktmax - 1))
                    p.recip(rd_sb[0:1, :], ps_o[64:65, :])
                    p.copy(rdb[0:1, :], rd_sb[0:1, :])
                    p.mm(ps_bc[0:64, :], onesb[:, :], rdb[0:1, :], start=True, stop=True)
                    p.copy(bc_sb[:, :], ps_bc[0:64, :])
                    p.tt(outT[64 * (hh % 2):64 * (hh % 2) + 64,
                              (hh // 2) * 2048 + qc * 512:(hh // 2) * 2048 + qc * 512 + 512],
                         ps_o[0:64, :], bc_sb[:, :], ALU.mult)
        for qc in range(4):
            for sub in range(4):
                t0 = base + qc * 512 + sub * 128
                for nch in range(2):
                    for kt in range(8):
                        p.mm(ps_a[:, :],
                             outT[:, kt * 2048 + qc * 512 + sub * 128:
                                  kt * 2048 + qc * 512 + sub * 128 + 128],
                             wout_sb[:, kt * 1024 + nch * 512:kt * 1024 + nch * 512 + 512],
                             start=(kt == 0), stop=(kt == 7))
                    p.copy(otile[:, nch * 512:(nch + 1) * 512], ps_a[:, :])
                p.dma_in(xstage[:], xs[t0:t0 + 128, :])
                p.tt(otile[:], otile[:], xstage[:], ALU.add)
                p.dma_out(ys[t0:t0 + 128, :], otile[:])
    p.finalize()

    def run(x, inp):
        wq = np.ascontiguousarray(np.asarray(inp["L1_attn_a_Wq"]).reshape(1024, 1024))
        wkvd = np.ascontiguousarray(np.asarray(inp["L1_attn_a_Wkv_down"]))
        wkup = np.ascontiguousarray(np.asarray(inp["L1_attn_a_Wk_up"]).reshape(256, 1024))
        wvup_r = np.asarray(inp["L1_attn_a_Wv_up"]).reshape(256, 1024)
        vones = np.zeros((1, 520), np.float32)
        wvup = np.zeros((256, 1040), np.float32)
        for h in range(16):
            half, hl = divmod(h, 8)
            wvup[:, 520 * half + 65 * hl:520 * half + 65 * hl + 64] = \
                wvup_r[:, 64 * h:64 * h + 64]
        for hl in range(8):
            vones[0, 65 * hl + 64] = 1.0
        vones_h = np.tile(vones, (128, 1)).astype(BFNP)
        wout = np.ascontiguousarray(np.asarray(inp["L1_attn_a_Wout"]).reshape(1024, 1024))
        kp = np.arange(128)[:, None]
        j = np.arange(896)[None, :]
        maskb = np.where(j >= kp + 384, 0.0, NEG).astype(BFNP)
        in_maps = [{"xs": x[c * TPC:(c + 1) * TPC], "wq": wq, "wkvd": wkvd,
                    "wkup": wkup, "wvup": wvup, "wout": wout, "vones": vones_h,
                    "maskb": maskb, "identb": IDENTB} for c in range(NCORES)]
        res = _launch("mla", nc, in_maps)
        return np.concatenate([np.asarray(r["ys"]) for r in res.results], axis=0)

    return run



def build_wattn():
    W = 128
    p = Prog("wattn")
    nc = p.nc
    xs = nc.dram_tensor("xs", [TPC, D], F32, kind="ExternalInput").ap()
    wqkvd = nc.dram_tensor("wqkv", [1024, 3072], F32, kind="ExternalInput").ap()
    woutd = nc.dram_tensor("wout", [1024, 1024], F32, kind="ExternalInput").ap()
    boutd = nc.dram_tensor("bout", [128, 1024], F32, kind="ExternalInput").ap()
    maskwd = nc.dram_tensor("maskw", [128, 128], BF16, kind="ExternalInput").ap()
    identbd = nc.dram_tensor("identb", [128, 128], BF16, kind="ExternalInput").ap()
    ys = nc.dram_tensor("ys", [TPC, D], F32, kind="ExternalOutput").ap()

    xstage = p.sb("xstage", 128, 1024, F32)
    xc = p.sb("xc", 128, 1024, F32)
    sq = p.sb("sq", 128, 1024, F32)
    st = p.sb("st", 128, 8, F32)
    xn = p.sb("xn", 128, 1024, BF16)
    xnTt = p.sb("xnTt", 128, 8 * 128, BF16)
    q_sb = p.sb("q_sb", 128, 16 * 1024, BF16)
    k_sb = p.sb("k_sb", 128, 16 * 1025, BF16)
    zk = p.sb("zk", 128, 1025, BF16)
    zkT = p.sb("zkT", 128, 8 * 128, BF16)
    qT = p.sb("qT", 128, 8 * 128, BF16)
    kT = [p.sb("kT0", 128, 8 * 128, BF16), p.sb("kT1", 128, 8 * 128, BF16)]
    pTa = p.sb("pTa", 128, 128, BF16)
    pTb = p.sb("pTb", 128, 128, BF16)
    outTw = p.sb("outTw", 128, 8 * 128, BF16)
    wqkv_sb = p.sb("wqkv_sb", 128, 8 * 2048, BF16)
    wout_sb = p.sb("wout_sb", 128, 8 * 1024, BF16)
    bout = p.sb("bout_s", 128, 1024, F32)
    maskw = p.sb("maskw_s", 128, 128, BF16)
    wstage = p.sb("wstage", 128, 2048, F32)
    rd_sb = p.sb("rd_sb", 1, 128, F32)
    rdb = p.sb("rdb", 1, 128, BF16)
    onesb = p.sb("onesb", 1, 128, BF16)
    bc_sb = p.sb("bc_sb", 128, 128, BF16)
    otile = p.sb("otile", 128, 1024, F32)
    identb = p.sb("identb_s", 128, 128, BF16)

    ps_a = p.ps("ps_a")
    ps_b = p.ps("ps_b")
    ps_s = [p.ps("ps_s0"), p.ps("ps_s1")]
    ps_o = p.ps("ps_o")
    ps_dn = p.ps("ps_dn")
    ps_bc = p.ps("ps_bc")
    psT = p.ps("psT0", 1024, BF16)

    p.dma_in(identb[:], identbd)
    p.dma_in(maskw[:], maskwd)
    p.dma_in(bout[:], boutd)
    p.memset(onesb[:], 1.0)
    p.memset(zk[:], 0.0)
    p.memset(zk[:, 1024:1025], 1.0)
    p.memset(zkT[:], 0.0)
    for kt in range(8):
        p.dma_in(wstage[:, 0:2048], wqkvd[kt * 128:(kt + 1) * 128, 0:2048])
        p.copy(wqkv_sb[:, kt * 2048:(kt + 1) * 2048], wstage[:, 0:2048])
    for kt in range(8):
        p.dma_in(wstage[:, 0:1024], woutd[kt * 128:(kt + 1) * 128, :])
        p.copy(wout_sb[:, kt * 1024:(kt + 1) * 1024], wstage[:, 0:1024])

    for b in range(BPC):
        base = b * S
        for tile in range(16):
            t0 = base + tile * 128
            p.dma_in(xstage[:], xs[t0:t0 + 128, :])
            ln_into(p, xstage[:], xc[:], sq[:], st[:], xn[:])
            for kt in range(8):
                pt = psT[:, 0:128] if kt % 2 == 0 else psT[:, 128:256]
                p.trans(pt, xn[:, kt * 128:(kt + 1) * 128], identb[:])
                p.copy(xnTt[:, kt * 128:(kt + 1) * 128], pt)
            for nch in range(2):
                for kt in range(8):
                    p.mm(ps_a[:, :], xnTt[:, kt * 128:(kt + 1) * 128],
                         wqkv_sb[:, kt * 2048 + nch * 512:kt * 2048 + nch * 512 + 512],
                         start=(kt == 0), stop=(kt == 7))
                p.copy(q_sb[:, tile * 1024 + nch * 512:tile * 1024 + nch * 512 + 512],
                       ps_a[:, :])
                for kt in range(8):
                    p.mm(ps_b[:, :], xnTt[:, kt * 128:(kt + 1) * 128],
                         wqkv_sb[:, kt * 2048 + 1024 + nch * 512:
                                  kt * 2048 + 1024 + nch * 512 + 512],
                         start=(kt == 0), stop=(kt == 7))
                p.copy(k_sb[:, tile * 1025 + nch * 512:tile * 1025 + nch * 512 + 512],
                       ps_b[:, :])
            p.memset(k_sb[:, tile * 1025 + 1024:tile * 1025 + 1025], 1.0)
        for w in range(16):
            t0 = base + w * 128
            for kt in range(8):
                pt = psT[:, 256:384] if kt % 2 == 0 else psT[:, 384:512]
                p.trans(pt, q_sb[:, w * 1024 + kt * 128:w * 1024 + kt * 128 + 128], identb[:])
                p.copy(qT[:, kt * 128:(kt + 1) * 128], pt)
                pt2 = psT[:, 512:640] if kt % 2 == 0 else psT[:, 640:768]
                p.trans(pt2, k_sb[:, w * 1025 + kt * 128:w * 1025 + kt * 128 + 128], identb[:])
                p.copy(kT[w % 2][:, kt * 128:(kt + 1) * 128], pt2)
            kTprev = zkT if w == 0 else kT[(w - 1) % 2]
            kprev_cols = zk[:, 0:1025] if w == 0 else k_sb[:, (w - 1) * 1025:w * 1025]
            kcur_cols = k_sb[:, w * 1025:(w + 1) * 1025]
            for kt in range(8):
                p.mm(ps_s[0][:, 0:128], kTprev[:, kt * 128:(kt + 1) * 128],
                     qT[:, kt * 128:(kt + 1) * 128], start=(kt == 0), stop=(kt == 7))
            for kt in range(8):
                p.mm(ps_s[1][:, 0:128], kT[w % 2][:, kt * 128:(kt + 1) * 128],
                     qT[:, kt * 128:(kt + 1) * 128], start=(kt == 0), stop=(kt == 7))
            p.tt(ps_s[1][:, 0:128], ps_s[1][:, 0:128], maskw[:], ALU.add)
            p.act(pTa[:], ps_s[0][:, 0:128], AF.Exp, scale=1.0 / 32.0)
            p.act(pTb[:], ps_s[1][:, 0:128], AF.Exp, scale=1.0 / 32.0)
            p.mm(ps_dn[0:1, 0:128], kprev_cols[:, 1024:1025], pTa[:], start=True, stop=False)
            p.mm(ps_dn[0:1, 0:128], kcur_cols[:, 1024:1025], pTb[:], start=False, stop=True)
            p.recip(rd_sb[0:1, :], ps_dn[0:1, 0:128])
            p.copy(rdb[0:1, :], rd_sb[0:1, :])
            p.mm(ps_bc[:, 0:128], onesb[:, :], rdb[0:1, :], start=True, stop=True)
            p.copy(bc_sb[:, :], ps_bc[:, 0:128])
            for m in range(8):
                p.mm(ps_o[:, 0:128], kprev_cols[:, m * 128:(m + 1) * 128], pTa[:],
                     start=True, stop=False)
                p.mm(ps_o[:, 0:128], kcur_cols[:, m * 128:(m + 1) * 128], pTb[:],
                     start=False, stop=True)
                p.tt(outTw[:, m * 128:(m + 1) * 128], ps_o[:, 0:128], bc_sb[:], ALU.mult)
            for nch in range(2):
                for m in range(8):
                    p.mm(ps_a[:, :], outTw[:, m * 128:(m + 1) * 128],
                         wout_sb[:, m * 1024 + nch * 512:m * 1024 + nch * 512 + 512],
                         start=(m == 0), stop=(m == 7))
                p.tt(otile[:, nch * 512:(nch + 1) * 512], ps_a[:, :],
                     bout[:, nch * 512:(nch + 1) * 512], ALU.add)
            p.dma_in(xstage[:], xs[t0:t0 + 128, :])
            p.tt(otile[:], otile[:], xstage[:], ALU.add)
            p.dma_out(ys[t0:t0 + 128, :], otile[:])
    p.finalize()

    def run(x, inp):
        wqkv = np.ascontiguousarray(np.asarray(inp["L1_attn_b_w_qkv"]))
        wout = np.ascontiguousarray(np.asarray(inp["L1_attn_b_w_out"]))
        bout = np.tile(np.asarray(inp["L1_attn_b_b_out"])[None, :], (128, 1)).astype(np.float32)
        jj = np.arange(128)[None, :]
        kk = np.arange(128)[:, None]
        maskw = np.where(kk > jj, NEG, 0.0).astype(BFNP)
        in_maps = [{"xs": x[c * TPC:(c + 1) * TPC], "wqkv": wqkv, "wout": wout,
                    "bout": bout, "maskw": maskw, "identb": IDENTB}
                   for c in range(NCORES)]
        res = _launch("wattn", nc, in_maps)
        return np.concatenate([np.asarray(r["ys"]) for r in res.results], axis=0)

    return run



def kernel(**inp):
    x = np.ascontiguousarray(np.asarray(inp["L0_attn_a_x"], np.float32).reshape(T, D))
    mingru = get_prog("mingru", build_mingru)
    moe0 = get_prog("moe0", lambda: build_moe(0))
    moe1 = get_prog("moe1", lambda: build_moe(1))
    moesum = get_prog("moesum", build_moesum)
    attn0 = get_prog("attn0", build_attn0)
    mla = get_prog("mla", build_mla)
    wattn = get_prog("wattn", build_wattn)

    x = mingru(x, inp)
    parts = moe0(x, inp, "L0_ffn")
    x = moesum(x, parts)
    x = attn0(x, inp)
    parts = moe0(x, inp, "L0_ffn")
    x = moesum(x, parts)
    x = mla(x, inp)
    parts = moe1(x, inp, "L1_ffn")
    x = moesum(x, parts)
    x = wattn(x, inp)
    parts = moe1(x, inp, "L1_ffn")
    x = moesum(x, parts)
    return x.reshape(B, S, D)
